# version 11
# baseline (speedup 1.0000x reference)
import numpy as np
from contextlib import ExitStack

import concourse.bass as bass
import concourse.mybir as mybir
from concourse.bass_utils import run_bass_kernel_spmd

F32 = mybir.dt.float32
BF16 = mybir.dt.bfloat16
AF = mybir.ActivationFunctionType
ALU = mybir.AluOpType
AX = mybir.AxisListType

P = 128
NSEQ = 4
T = 2048
NTT = T // P
D = 1024
NDC = D // P
F = 2816
NFC = F // P
NEXP = 8
NG = 4
G = 512
FS = [(0, 8), (8, 8), (16, 6)]
NCORES = 8

ENGINES = ("tensor", "vector", "scalar", "sync", "gpsimd")


class Rec:

    def __init__(self):
        self.items = {e: [] for e in ENGINES}
        self.cnt = {}
        self.step = {}

    def sem(self, name, step=1):
        self.cnt[name] = 0
        self.step[name] = step

    def op(self, eng, fn, inc=None, n=1):
        if inc is not None:
            self.cnt[inc] += self.step[inc] * n
        self.items[eng].append(("op", fn, inc, self.step[inc] * n if inc else 0))
        return self.cnt[inc] if inc else None

    def wait(self, eng, sname, val=None):
        v = self.cnt[sname] if val is None else val
        if v > 0:
            self.items[eng].append(("wait", sname, v))

    def emit(self, block, sems):
        hw = {e: {s: 0 for s in self.cnt} for e in ENGINES}

        def body(eng_name):
            def run(E):
                for it in self.items[eng_name]:
                    if it[0] == "wait":
                        _, s, v = it
                        if v > hw[eng_name][s]:
                            E.wait_ge(sems[s], v)
                            hw[eng_name][s] = v
                    else:
                        _, fn, inc, k = it
                        instr = fn(E)
                        if inc is not None:
                            instr.then_inc(sems[inc], k)
            return run

        block.tensor(body("tensor"))
        block.vector(body("vector"))
        block.scalar(body("scalar"))
        block.sync(body("sync"))
        block.gpsimd(body("gpsimd"))


class Banks:

    def __init__(self, R):
        self.R = R
        self.rel = [{"sdv": 0, "sac": 0} for _ in range(4)]

    def _banks(self, c0, c1):
        return range(c0 // 512, (c1 + 511) // 512)

    def reader(self, c0, c1, sem, val):
        for b in self._banks(c0, c1):
            self.rel[b][sem] = max(self.rel[b][sem], val)

    def write_wait(self, eng, c0, c1):
        for b in self._banks(c0, c1):
            for sem in ("sdv", "sac"):
                self.R.wait(eng, sem, self.rel[b][sem])


def build_program(DBG=False):
    nc = bass.Bass("TRN2", target_bir_lowering=False, debug=False)
    x = nc.dram_tensor("x", [NSEQ, T, D], F32, kind="ExternalInput").ap()
    wqT = nc.dram_tensor("wqT", [D, D], F32, kind="ExternalInput").ap()
    wkT = nc.dram_tensor("wkT", [D, D], F32, kind="ExternalInput").ap()
    wvT = nc.dram_tensor("wvT", [D, D], F32, kind="ExternalInput").ap()
    bqc = nc.dram_tensor("bqc", [P, NDC], F32, kind="ExternalInput").ap()
    bkc = nc.dram_tensor("bkc", [P, NDC], F32, kind="ExternalInput").ap()
    bvr = nc.dram_tensor("bvr", [1, D], F32, kind="ExternalInput").ap()
    rw = nc.dram_tensor("rw", [P, NDC, NEXP], F32, kind="ExternalInput").ap()
    gate = nc.dram_tensor("gate", [NEXP, D, F], F32, kind="ExternalInput").ap()
    up = nc.dram_tensor("up", [NEXP, D, F], F32, kind="ExternalInput").ap()
    down = nc.dram_tensor("down", [NEXP, F, D], F32, kind="ExternalInput").ap()
    ident = nc.dram_tensor("ident", [P, P], F32, kind="ExternalInput").ap()
    maskd = nc.dram_tensor("maskd", [P, P], F32, kind="ExternalInput").ap()
    onesd = nc.dram_tensor("onesd", [1, P], F32, kind="ExternalInput").ap()
    y = nc.dram_tensor("y", [NSEQ, T, D], F32, kind="ExternalOutput").ap()
    NSLOT = NEXP * len(FS)
    gsc = nc.dram_tensor("gsc", [NSLOT, P, NDC, 1024], BF16, kind="Internal").ap()
    usc = nc.dram_tensor("usc", [NSLOT, P, NDC, 1024], BF16, kind="Internal").ap()
    dsc = nc.dram_tensor("dsc", [NSLOT, P, NDC, 1024], BF16, kind="Internal").ap()
    if DBG:
        d_xnt = nc.dram_tensor("d_xnt", [P, NDC, T], BF16, kind="ExternalOutput").ap()
        d_stats = nc.dram_tensor("d_stats", [P, 3, NTT], F32, kind="ExternalOutput").ap()
        d_ew = nc.dram_tensor("d_ew", [P, NTT, NEXP], F32, kind="ExternalOutput").ap()
        d_kt = nc.dram_tensor("d_kt", [P, NDC, T], BF16, kind="ExternalOutput").ap()
        d_vt = nc.dram_tensor("d_vt", [P, NTT, D], BF16, kind="ExternalOutput").ap()
        d_qg = nc.dram_tensor("d_qg", [P, NDC, G], BF16, kind="ExternalOutput").ap()
        d_pb = nc.dram_tensor("d_pb", [P, T], BF16, kind="ExternalOutput").ap()
        d_smx = nc.dram_tensor("d_smx", [P, 2, 4], F32, kind="ExternalOutput").ap()
        d_oa = nc.dram_tensor("d_oa", [P, NTT, D], F32, kind="ExternalOutput").ap()
        d_oaf = nc.dram_tensor("d_oaf", [P, NTT, D], F32, kind="ExternalOutput").ap()
        d_h2 = nc.dram_tensor("d_h2", [P, NDC, G], BF16, kind="ExternalOutput").ap()
        d_x = nc.dram_tensor("d_x", [P, NTT, D], F32, kind="ExternalOutput").ap()

    R = Rec()
    for sname in ("sx", "sy", "sw", "sl", "sb"):
        R.sem(sname, 16)
    for sname in ("spe", "sdv", "sac"):
        R.sem(sname, 1)

    with ExitStack() as ctx:
        sb = lambda name, shape, dt: ctx.enter_context(nc.sbuf_tensor(name, shape, dt))
        OA = sb("OA", [P, NTT, D], F32)
        XNT = sb("XNT", [P, NDC, T], BF16)
        sum16 = sb("sum16", [P, NTT], F32)
        ss16 = sb("ss16", [P, NTT], F32)
        mean16 = sb("mean16", [P, NTT], F32)
        var16 = sb("var16", [P, NTT], F32)
        rstd16 = sb("rstd16", [P, NTT], F32)
        nmr16 = sb("nmr16", [P, NTT], F32)
        tmp16 = sb("tmp16", [P, NTT], F32)
        EW = sb("EW", [P, NTT, NEXP], F32)
        EQ1 = sb("EQ1", [P, NTT, NEXP], BF16)
        EQ2 = sb("EQ2", [P, NTT, NEXP], BF16)
        DLT = sb("DLT", [P, NTT], F32)
        W1 = sb("W1", [P, NTT], F32)
        W2 = sb("W2", [P, NTT], F32)
        lgt = sb("lgt", [P, NEXP], F32)
        l2b = sb("l2b", [P, NEXP], F32)
        m1b = sb("m1b", [P, 1], F32)
        m2b = sb("m2b", [P, 1], F32)
        smx = sb("smx", [P, 2, 4], F32)
        nmx = sb("nmx", [P, 2, 4], F32)
        scn = sb("scn", [P, 2, 4], F32)
        identT = sb("identT", [P, P], BF16)
        maskT = sb("maskT", [P, P], BF16)
        RW = sb("RW", [P, NDC, NEXP], BF16)
        bqs = sb("bqs", [P, NDC], F32)
        bks = sb("bks", [P, NDC], F32)
        WINB = 108544
        arena = ctx.enter_context(nc.sbuf_tensor("win_arena", [P, WINB], mybir.dt.uint8))
        abase = nc.lookup_mloc(arena).addr
        at = lambda name, shape, dt, off: nc.alloc_sbuf_tensor_at(
            name, shape, dt, offset=abase + off)
        KT = at("KT", [P, NDC, T], BF16, 0)
        W8b = at("W8b", [P, NDC, D], BF16, 81920)
        VT = at("VT", [P, NTT, D], BF16, 32768)
        W8 = at("W8", [P, NDC, D], BF16, 65536)
        QG = at("QG", [P, NDC, G], BF16, 81920)
        Pb = at("Pb", [P, T], BF16, 90112)
        PT = at("PT", [P, NTT, P], BF16, 94208)
        G2 = at("G2", [P, 2, NDC, 1024], BF16, 0)
        U2 = at("U2", [P, 2, NDC, 1024], BF16, 32768)
        D2 = at("D2", [P, 2, NDC, 1024], BF16, 65536)
        H2 = at("H2", [P, NDC, G], BF16, 98304)
        SIL = at("SIL", [P, 2, 512], BF16, 106496)
        xn_tok = at("xn_tok", [P, 2, D], BF16, 0)
        dumpA = at("dumpA", [P, D], BF16, 4096)
        bvs = at("bvs", [1, D], BF16, 98304)
        ones1 = at("ones1", [1, P], BF16, 100352)
        psA = ctx.enter_context(nc.psum_tensor("psA", [P, 2048], F32))
        psB = ctx.enter_context(nc.psum_tensor("psB", [P, 2048], F32))
        psAb = psA.bitcast(BF16)

        sems = {}
        for sname in R.cnt:
            sems[sname] = ctx.enter_context(nc.semaphore(sname))

        BA = Banks(R)
        BB = Banks(R)

        R.op("gpsimd", lambda g: g.dma_start(out=identT[:], in_=ident), inc="sl")
        R.op("gpsimd", lambda g: g.dma_start(out=RW[:], in_=rw), inc="sl")
        R.op("gpsimd", lambda g: g.dma_start(out=maskT[:], in_=maskd), inc="sl")
        R.op("sync", lambda g: g.dma_start(out=bqs[:], in_=bqc), inc="sl")
        R.op("sync", lambda g: g.dma_start(out=bks[:], in_=bkc), inc="sl")
        consts_done = R.cnt["sl"]
        for e in ("tensor", "vector", "scalar"):
            R.wait(e, "sl", consts_done)

        ffn_pe_end = 0
        ffn_dv_end = 0
        ffn_ac_end = 0
        y_done = 0
        y_gd = {}
        wb_done = {}
        wb_total = 0

        for s in range(NSEQ):
            xgload = []
            for g in range(4):
                if s > 0:
                    R.wait("sync", "sy", y_gd.get(g, y_done))
                R.op("sync",
                     (lambda s, g: lambda gg: gg.dma_start(
                         out=OA[:, g * 4:(g + 1) * 4, :],
                         in_=x[s].rearrange("(tt p) d -> p tt d", p=P)
                         [:, g * 4:(g + 1) * 4, :]))(s, g),
                     inc="sx")
                xgload.append(R.cnt["sx"])
            xload = [xgload[tt // 4] for tt in range(NTT)]

            grp_nmr_dv = {}
            if DBG and s == 0:
                R.wait("sync", "sx", xload[-1])
                R.op("sync", lambda g: g.dma_start(out=d_x, in_=OA[:]), inc="sy")
            xnt_done = []
            rlogs = [psB[:, 1024:1024 + NEXP], psB[:, 1536:1536 + NEXP]]
            tr_pe_prev = 0
            rt_pe_by_tt = {}

            def top2_chain(tt):
                rlog = rlogs[tt % 2]
                c0 = 1024 + (tt % 2) * 512
                R.wait("vector", "spe", rt_pe_by_tt[tt])
                R.op("vector", (lambda rlog: lambda v: v.tensor_copy(lgt[:], rlog))(rlog),
                     inc="sdv")
                BB.reader(c0, c0 + NEXP, "sdv", R.cnt["sdv"])
                R.wait("vector", "sdv")
                R.op("vector", lambda v: v.reduce_max(m1b[:], lgt[:], axis=AX.X), inc="sdv")
                R.wait("vector", "sdv")
                R.op("vector",
                     (lambda tt: lambda v: v.tensor_scalar(
                         out=EQ1[:, tt, :], in0=lgt[:], scalar1=m1b[:], scalar2=None,
                         op0=ALU.is_equal, op1=ALU.bypass))(tt), inc="sdv")
                R.wait("vector", "sdv")
                R.op("vector",
                     (lambda tt: lambda v: v.scalar_tensor_tensor(
                         out=l2b[:], in0=EQ1[:, tt, :], scalar=-1e9, in1=lgt[:],
                         op0=ALU.mult, op1=ALU.add))(tt), inc="sdv")
                R.wait("vector", "sdv")
                R.op("vector", lambda v: v.reduce_max(m2b[:], l2b[:], axis=AX.X), inc="sdv")
                R.wait("vector", "sdv")
                R.op("vector",
                     (lambda tt: lambda v: v.tensor_scalar(
                         out=EQ2[:, tt, :], in0=l2b[:], scalar1=m2b[:], scalar2=None,
                         op0=ALU.is_equal, op1=ALU.bypass))(tt), inc="sdv")
                R.op("vector",
                     (lambda tt: lambda v: v.tensor_tensor(
                         out=DLT[:, tt:tt + 1], in0=m2b[:], in1=m1b[:],
                         op=ALU.subtract))(tt), inc="sdv")

            def a_tile_pipeline(tt):
                nonlocal tr_pe_prev
                par = tt % 2
                R.wait("scalar", "sdv", grp_nmr_dv[tt // 4])
                if tt >= 2:
                    R.wait("scalar", "spe", tr_pe_prev)
                R.op("scalar",
                     (lambda tt, par: lambda a: a.activation(
                         xn_tok[:, par, :], OA[:, tt, :], AF.Identity,
                         scale=rstd16[:, tt:tt + 1],
                         bias=nmr16[:, tt:tt + 1]))(tt, par),
                     inc="sac")
                xn_dv = R.cnt["sac"]
                R.wait("tensor", "sac", xn_dv)
                BA.write_wait("tensor", 0, 512)
                for c in range(NDC):
                    R.op("tensor",
                         (lambda c, par: lambda t: t.transpose(
                             psAb[:, c * P:(c + 1) * P],
                             xn_tok[:, par, c * P:(c + 1) * P], identT[:]))(c, par),
                         inc="spe" if c == NDC - 1 else None)
                tr_pe_prev = R.cnt["spe"]
                R.wait("vector", "spe", tr_pe_prev)
                R.op("vector",
                     (lambda tt: lambda v: v.tensor_copy(
                         XNT[:, :, tt * P:(tt + 1) * P],
                         psAb[:, 0:NDC * P].rearrange("p (c n) -> p c n", c=NDC)))(tt),
                     inc="sdv")
                xnt_done.append(("sdv", R.cnt["sdv"]))
                BA.reader(0, 512, "sdv", R.cnt["sdv"])
                R.wait("tensor", xnt_done[-1][0], xnt_done[-1][1])
                c0 = 1024 + (tt % 2) * 512
                BB.write_wait("tensor", c0, c0 + NEXP)
                for c in range(NDC):
                    R.op("tensor",
                         (lambda c, tt, rlog: lambda t: t.matmul(
                             rlog, XNT[:, c, tt * P:(tt + 1) * P], RW[:, c, :],
                             start=(c == 0), stop=(c == NDC - 1)))(c, tt, rlogs[tt % 2]),
                         inc="spe" if c == NDC - 1 else None)
                rt_pe_by_tt[tt] = R.cnt["spe"]
                if tt >= 1:
                    top2_chain(tt - 1)

            for gA in range(4):
                g0 = gA * 4
                for tt in range(g0, g0 + 4):
                    dumpv = XNT[:, :, tt * P:(tt + 1) * P]
                    R.wait("vector", "sx", xgload[gA])
                    R.op("vector",
                         (lambda tt, dumpv: lambda v: v.scalar_tensor_tensor(
                             out=dumpv, in0=OA[:, tt, :], scalar=1.0, in1=OA[:, tt, :],
                             op0=ALU.bypass, op1=ALU.mult,
                             accum_out=ss16[:, tt:tt + 1]))(tt, dumpv),
                         inc="sdv")
                    R.wait("scalar", "sx", xgload[gA])
                    R.op("scalar",
                         (lambda tt: lambda a: a.activation(
                             dumpA[:], OA[:, tt, :], AF.Identity,
                             accum_out=sum16[:, tt:tt + 1]))(tt),
                         inc="sac")
                sl4 = slice(g0, g0 + 4)
                R.wait("vector", "sdv")
                R.wait("vector", "sac")
                R.op("vector", (lambda sl4: lambda v: v.tensor_scalar(
                    out=mean16[:, sl4], in0=sum16[:, sl4], scalar1=1.0 / D, scalar2=None,
                    op0=ALU.mult, op1=ALU.bypass))(sl4), inc="sdv")
                R.wait("vector", "sdv")
                R.op("vector", (lambda sl4: lambda v: v.scalar_tensor_tensor(
                    out=tmp16[:, sl4], in0=mean16[:, sl4], scalar=1.0, in1=mean16[:, sl4],
                    op0=ALU.bypass, op1=ALU.mult))(sl4), inc="sdv")
                R.op("vector", (lambda sl4: lambda v: v.tensor_scalar(
                    out=var16[:, sl4], in0=ss16[:, sl4], scalar1=1.0 / D, scalar2=None,
                    op0=ALU.mult, op1=ALU.bypass))(sl4), inc="sdv")
                R.wait("vector", "sdv")
                R.op("vector", (lambda sl4: lambda v: v.tensor_tensor(
                    out=var16[:, sl4], in0=var16[:, sl4], in1=tmp16[:, sl4],
                    op=ALU.subtract))(sl4), inc="sdv")
                R.wait("vector", "sdv")
                R.op("vector", (lambda sl4: lambda v: v.tensor_scalar(
                    out=var16[:, sl4], in0=var16[:, sl4], scalar1=1e-5, scalar2=None,
                    op0=ALU.add, op1=ALU.bypass))(sl4), inc="sdv")
                R.wait("scalar", "sdv")
                R.op("scalar", (lambda sl4: lambda a: a.activation(
                    tmp16[:, sl4], var16[:, sl4], AF.Ln))(sl4), inc="sac")
                R.wait("scalar", "sac")
                R.op("scalar", (lambda sl4: lambda a: a.activation(
                    rstd16[:, sl4], tmp16[:, sl4], AF.Exp, scale=-0.5))(sl4), inc="sac")
                R.wait("vector", "sac")
                R.op("vector", (lambda sl4: lambda v: v.scalar_tensor_tensor(
                    out=nmr16[:, sl4], in0=mean16[:, sl4], scalar=-1.0, in1=rstd16[:, sl4],
                    op0=ALU.mult, op1=ALU.mult))(sl4), inc="sdv")
                grp_nmr_dv[gA] = R.cnt["sdv"]
                for tt in range(g0, g0 + 4):
                    a_tile_pipeline(tt)
            top2_chain(NTT - 1)
            R.wait("scalar", "sdv")
            R.op("scalar", lambda a: a.activation(W1[:], DLT[:], AF.Exp), inc="sac")
            R.wait("vector", "sac")
            R.op("vector", lambda v: v.tensor_scalar(
                out=W1[:], in0=W1[:], scalar1=1.0, scalar2=None,
                op0=ALU.add, op1=ALU.bypass), inc="sdv")
            R.wait("vector", "sdv")
            R.op("vector", lambda v: v.reciprocal(W1[:], W1[:]), inc="sdv")
            R.wait("vector", "sdv")
            R.op("vector", lambda v: v.tensor_scalar(
                out=W2[:], in0=W1[:], scalar1=-1.0, scalar2=1.0,
                op0=ALU.mult, op1=ALU.add), inc="sdv")
            R.wait("vector", "sdv")
            for tt in range(NTT):
                R.op("vector",
                     (lambda tt: lambda v: v.tensor_scalar(
                         out=EW[:, tt, :], in0=EQ1[:, tt, :], scalar1=W1[:, tt:tt + 1],
                         scalar2=None, op0=ALU.mult, op1=ALU.bypass))(tt), inc="sdv")
                R.wait("vector", "sdv")
                R.op("vector",
                     (lambda tt: lambda v: v.scalar_tensor_tensor(
                         out=EW[:, tt, :], in0=EQ2[:, tt, :], scalar=W2[:, tt:tt + 1],
                         in1=EW[:, tt, :], op0=ALU.mult, op1=ALU.add))(tt), inc="sdv")
                R.wait("vector", "sdv")
            a_end_dv = R.cnt["sdv"]
            if DBG and s == 0:
                R.wait("sync", "sdv", a_end_dv)
                R.op("sync", lambda g: g.dma_start(out=d_xnt, in_=XNT[:]), inc="sy")
                R.op("sync", lambda g: g.dma_start(out=d_stats[:, 0, :], in_=mean16[:]), inc="sy")
                R.op("sync", lambda g: g.dma_start(out=d_stats[:, 1, :], in_=rstd16[:]), inc="sy")
                R.op("sync", lambda g: g.dma_start(out=d_stats[:, 2, :], in_=var16[:]), inc="sy")
                R.op("sync", lambda g: g.dma_start(out=d_ew, in_=EW[:]), inc="sy")
                y_done = R.cnt["sy"]

            if True:
                R.wait("gpsimd", "spe", ffn_pe_end)
                R.wait("gpsimd", "sdv", ffn_dv_end)
                R.wait("gpsimd", "sac", ffn_ac_end)
                R.wait("gpsimd", "sl", consts_done)
                if s == 1:
                    R.wait("gpsimd", "sb", wb_total)
                R.op("gpsimd", lambda g: g.dma_start(
                    out=W8[:], in_=wkT.rearrange("(c p) e -> p c e", p=P)), inc="sw")
                R.op("gpsimd", lambda g: g.dma_start(out=bvs[:], in_=bvr), inc="sw")
                R.op("gpsimd", lambda g: g.dma_start(out=ones1[:], in_=onesd), inc="sw")
                R.op("gpsimd", lambda g: g.dma_start(
                    out=W8b[:], in_=wvT.rearrange("(c p) e -> p c e", p=P)), inc="sw")
                wk_ld = R.cnt["sw"]
                wv_ld = wk_ld
                R.wait("tensor", "sw", wk_ld)
                for ec in range(NDC):
                    for kb in range(NG):
                        R.wait("tensor", xnt_done[kb * 4 + 3][0], xnt_done[kb * 4 + 3][1])
                        par = (ec * NG + kb) % 2
                        pb = psB[:, par * 512:(par + 1) * 512]
                        BB.write_wait("tensor", par * 512, (par + 1) * 512)
                        for c in range(NDC):
                            R.op("tensor",
                                 (lambda c, ec, kb, pb: lambda t: t.matmul(
                                     pb, W8[:, c, ec * P:(ec + 1) * P],
                                     XNT[:, c, kb * G:(kb + 1) * G],
                                     start=(c == 0), stop=(c == NDC - 1)))(c, ec, kb, pb),
                                 inc="spe" if c == NDC - 1 else None)
                        R.wait("scalar", "spe")
                        R.op("scalar",
                             (lambda ec, kb, pb: lambda a: a.activation(
                                 KT[:, ec, kb * G:(kb + 1) * G], pb, AF.Identity,
                                 bias=bks[:, ec:ec + 1]))(ec, kb, pb), inc="sac")
                        BB.reader(par * 512, (par + 1) * 512, "sac", R.cnt["sac"])
                k_pe_end = R.cnt["spe"]
                R.wait("gpsimd", "spe", k_pe_end)
                R.op("gpsimd", lambda g: g.dma_start(
                    out=W8[:], in_=wqT.rearrange("(c p) e -> p c e", p=P)), inc="sw")
                wq_ld = R.cnt["sw"]

                R.wait("tensor", "sw", wv_ld)
                for tt in range(NTT):
                    par = tt % 2
                    pv = psB[:, par * D:(par + 1) * D]
                    BB.write_wait("tensor", par * D, (par + 1) * D)
                    for h in range(2):
                        for c in range(NDC):
                            R.op("tensor",
                                 (lambda c, tt, h, pv: lambda t: t.matmul(
                                     pv[:, h * 512:(h + 1) * 512],
                                     XNT[:, c, tt * P:(tt + 1) * P],
                                     W8b[:, c, h * 512:(h + 1) * 512],
                                     start=(c == 0), stop=False))(c, tt, h, pv),
                                 inc=None)
                        R.op("tensor",
                             (lambda tt, h, pv: lambda t: t.matmul(
                                 pv[:, h * 512:(h + 1) * 512], ones1[:],
                                 bvs[:, h * 512:(h + 1) * 512],
                                 start=False, stop=True))(tt, h, pv),
                             inc="spe" if h == 1 else None)
                    R.wait("vector", "spe")
                    R.op("vector",
                         (lambda tt, pv: lambda v: v.tensor_copy(VT[:, tt, :], pv))(tt, pv),
                         inc="sdv")
                    BB.reader(par * D, (par + 1) * D, "sdv", R.cnt["sdv"])
                v_pe_end = R.cnt["spe"]
                if DBG and s == 0:
                    R.wait("sync", "sac")
                    R.wait("sync", "sdv")
                    R.op("sync", lambda g: g.dma_start(out=d_kt, in_=KT[:]), inc="sy")
                    R.op("sync", lambda g: g.dma_start(out=d_vt, in_=VT[:]), inc="sy")
                    y_done = R.cnt["sy"]

                R.wait("tensor", "sw", wq_ld)
                for g in range(NG):
                    for ec in range(NDC):
                        par = ec % 2
                        pq = psA[:, par * 512:(par + 1) * 512]
                        BA.write_wait("tensor", par * 512, (par + 1) * 512)
                        for c in range(NDC):
                            R.op("tensor",
                                 (lambda c, ec, g, pq: lambda t: t.matmul(
                                     pq, W8[:, c, ec * P:(ec + 1) * P],
                                     XNT[:, c, g * G:(g + 1) * G],
                                     start=(c == 0), stop=(c == NDC - 1)))(c, ec, g, pq),
                                 inc="spe" if c == NDC - 1 else None)
                        R.wait("scalar", "spe")
                        R.op("scalar",
                             (lambda ec, pq: lambda a: a.activation(
                                 QG[:, ec, :], pq, AF.Identity,
                                 bias=bqs[:, ec:ec + 1]))(ec, pq), inc="sac")
                        BA.reader(par * 512, (par + 1) * 512, "sac", R.cnt["sac"])
                    q_ready = R.cnt["sac"]

                    for j in range(4):
                        Tq = g * 4 + j
                        klen = (Tq + 1) * P
                        nb = (klen + 511) // 512
                        par = Tq % 2
                        R.wait("tensor", "sac", q_ready)
                        BA.write_wait("tensor", 0, klen)
                        blk_pe = []
                        for kb in range(nb):
                            ncols = min(512, klen - kb * 512)
                            is_diag = (kb == nb - 1)
                            for c in range(NDC):
                                R.op("tensor",
                                     (lambda c, kb, ncols, g, j, is_diag: lambda t: t.matmul(
                                         psA[:, kb * 512:kb * 512 + ncols],
                                         QG[:, c, j * P:(j + 1) * P],
                                         KT[:, c, kb * 512:kb * 512 + ncols],
                                         start=(c == 0),
                                         stop=(c == NDC - 1 and not is_diag)))(c, kb, ncols, g, j, is_diag),
                                     inc="spe" if (c == NDC - 1 and not is_diag) else None)
                            if is_diag:
                                R.op("tensor",
                                     (lambda Tq, kb, ncols: lambda t: t.matmul(
                                         psA[:, kb * 512 + ncols - P:kb * 512 + ncols],
                                         identT[:], maskT[:],
                                         start=False, stop=True))(Tq, kb, ncols),
                                     inc="spe")
                            blk_pe.append(R.cnt["spe"])
                        s_pe = R.cnt["spe"]
                        for kb in range(nb):
                            ncols = min(512, klen - kb * 512)
                            R.wait("vector", "spe", blk_pe[kb])
                            R.op("vector",
                                 (lambda kb, ncols, par: lambda v: v.reduce_max(
                                     nmx[:, par, kb:kb + 1],
                                     psA[:, kb * 512:kb * 512 + ncols], axis=AX.X,
                                     negate=True))(kb, ncols, par), inc="sdv")
                            BA.reader(kb * 512, kb * 512 + ncols, "sdv", R.cnt["sdv"])
                        R.wait("vector", "sdv")
                        R.op("vector",
                             (lambda nb, par: lambda v: v.tensor_scalar(
                                 out=scn[:, par, 0:nb], in0=nmx[:, par, 0:nb],
                                 scalar1=1.0 / 32.0, scalar2=None,
                                 op0=ALU.mult, op1=ALU.min,
                                 accum_out=smx[:, par, 1:2]))(nb, par), inc="sdv")
                        mx_dv = R.cnt["sdv"]
                        nkb = Tq + 1
                        exp_ac_blk = []
                        R.wait("scalar", "sdv", mx_dv)
                        for kb in range(nb):
                            ncols = min(512, klen - kb * 512)
                            R.op("scalar",
                                 (lambda kb, ncols, par: lambda a: a.activation(
                                     Pb[:, kb * 512:kb * 512 + ncols],
                                     psA[:, kb * 512:kb * 512 + ncols], AF.Exp,
                                     bias=smx[:, par, 1:2], scale=1.0 / 32.0,
                                     accum_out=nmx[:, par, kb:kb + 1]))(kb, ncols, par),
                                 inc="sac")
                            exp_ac_blk.append(R.cnt["sac"])
                            BA.reader(kb * 512, kb * 512 + ncols, "sac", R.cnt["sac"])
                        exp_ac = R.cnt["sac"]
                        R.wait("vector", "sac", exp_ac)
                        R.op("vector",
                             (lambda nb, par: lambda v: v.tensor_scalar(
                                 out=scn[:, par, 0:nb], in0=nmx[:, par, 0:nb],
                                 scalar1=0.0, scalar2=None,
                                 op0=ALU.add, op1=ALU.add,
                                 accum_out=smx[:, par, 2:3]))(nb, par), inc="sdv")
                        R.wait("vector", "sdv")
                        R.op("vector",
                             (lambda par: lambda v: v.reciprocal(
                                 smx[:, par, 3:4], smx[:, par, 2:3]))(par), inc="sdv")
                        recip_dv = R.cnt["sdv"]
                        BA.write_wait("tensor", 0, nkb * P // 2)
                        copy_dv = []
                        for kb in range(nkb):
                            R.wait("tensor", "sac", exp_ac_blk[kb // 4])
                            R.op("tensor",
                                 (lambda kb: lambda t: t.transpose(
                                     psAb[:, kb * P:(kb + 1) * P],
                                     Pb[:, kb * P:(kb + 1) * P], identT[:]))(kb),
                                 inc="spe" if (kb % 8 == 7 or kb == nkb - 1) else None)
                            if kb % 8 == 7 or kb == nkb - 1:
                                b0 = (kb // 8) * 8
                                cntk = kb + 1 - b0
                                R.wait("vector", "spe")
                                R.op("vector",
                                     (lambda b0, cntk: lambda v: v.tensor_copy(
                                         PT[:, b0:b0 + cntk, :],
                                         psAb[:, b0 * P:(b0 + cntk) * P]
                                         .rearrange("p (k n) -> p k n", n=P)))(b0, cntk),
                                     inc="sdv")
                                copy_dv.append(R.cnt["sdv"])
                                BA.reader(b0 * P // 2, (b0 + cntk) * P // 2, "sdv", R.cnt["sdv"])
                        tr_pe = R.cnt["spe"]
                        pav = psB[:, par * D:(par + 1) * D]
                        BB.write_wait("tensor", par * D, (par + 1) * D)
                        for kb in range(nkb):
                            if kb % 8 == 0:
                                R.wait("tensor", "sdv",
                                       copy_dv[min(kb // 8, len(copy_dv) - 1)])
                            for h in range(2):
                                R.op("tensor",
                                     (lambda kb, h, pav, nkb: lambda t: t.matmul(
                                         pav[:, h * 512:(h + 1) * 512], PT[:, kb, :],
                                         VT[:, kb, h * 512:(h + 1) * 512],
                                         start=(kb == 0), stop=(kb == nkb - 1)))(kb, h, pav, nkb),
                                     inc="spe" if (kb == nkb - 1 and h == 1) else None)
                        av_pe = R.cnt["spe"]
                        R.wait("vector", "spe", av_pe)
                        R.wait("vector", "sdv", recip_dv)
                        R.op("vector",
                             (lambda Tq, par, pav: lambda v: v.scalar_tensor_tensor(
                                 out=OA[:, Tq, :], in0=pav, scalar=smx[:, par, 3:4],
                                 in1=OA[:, Tq, :], op0=ALU.mult, op1=ALU.add))(Tq, par, pav),
                             inc="sdv")
                        BB.reader(par * D, (par + 1) * D, "sdv", R.cnt["sdv"])
                c_pe_end = R.cnt["spe"]
                c_dv_end = R.cnt["sdv"]
                c_ac_end = R.cnt["sac"]
                if DBG and s == 0:
                    R.wait("sync", "sdv", c_dv_end)
                    R.wait("sync", "sac", c_ac_end)
                    R.op("sync", lambda g: g.dma_start(out=d_qg, in_=QG[:]), inc="sy")
                    R.op("sync", lambda g: g.dma_start(out=d_pb, in_=Pb[:]), inc="sy")
                    R.op("sync", lambda g: g.dma_start(out=d_smx, in_=smx[:]), inc="sy")
                    R.op("sync", lambda g: g.dma_start(out=d_oa, in_=OA[:]), inc="sy")
                    y_done = R.cnt["sy"]

            if True:
                gud_release = {}
                y_group_done = {}
                hq = []
                sil_rel = {0: 0, 1: 0}
                slot = 0
                first_ffn = True
                w_ready_prev = 0
                for e in range(NEXP):
                    for (fs0, nfc) in FS:
                        par = slot % 2
                        if slot <= 1:
                            R.wait("gpsimd", "spe", c_pe_end)
                            R.wait("gpsimd", "sdv", c_dv_end)
                            R.wait("gpsimd", "sac", c_ac_end)
                            if DBG:
                                R.wait("gpsimd", "sy", y_done)
                        else:
                            R.wait("gpsimd", "spe", gud_release[slot - 2])
                        if s == 0 and slot >= 2:
                            R.wait("gpsimd", "sb", wb_done[slot - 2])
                        if s == 1 and slot <= 1:
                            R.wait("gpsimd", "sb", wb_total)
                        R.wait("gpsimd", "sw", w_ready_prev)
                        halves = ([(0, 2), (2, 4), (4, 6), (6, nfc)]
                                  if slot == 0 else [(0, nfc)])
                        w_half = []
                        for (h0, h1) in halves:
                            if s == 0:
                                R.op("gpsimd",
                                     (lambda e, fs0, par, h0, h1: lambda g: g.dma_start(
                                         out=G2[:, par, :, h0 * P:h1 * P],
                                         in_=gate[e].rearrange("(c p) f -> p c f", p=P)
                                         [:, :, (fs0 + h0) * P:(fs0 + h1) * P]))(e, fs0, par, h0, h1),
                                     inc="sw")
                                R.op("gpsimd",
                                     (lambda e, fs0, par, h0, h1: lambda g: g.dma_start(
                                         out=U2[:, par, :, h0 * P:h1 * P],
                                         in_=up[e].rearrange("(c p) f -> p c f", p=P)
                                         [:, :, (fs0 + h0) * P:(fs0 + h1) * P]))(e, fs0, par, h0, h1),
                                     inc="sw")
                                R.op("gpsimd",
                                     (lambda e, fs0, par, h0, h1: lambda g: g.dma_start(
                                         out=D2[:, par, h0:h1, :],
                                         in_=down[e].rearrange("(c p) d -> p c d", p=P)
                                         [:, fs0 + h0:fs0 + h1, :]))(e, fs0, par, h0, h1),
                                     inc="sw")
                            else:
                                R.op("gpsimd",
                                     (lambda sl_, par, h0, h1: lambda g: g.dma_start(
                                         out=G2[:, par, :, h0 * P:h1 * P],
                                         in_=gsc[sl_][:, :, h0 * P:h1 * P]))(slot, par, h0, h1),
                                     inc="sw")
                                R.op("gpsimd",
                                     (lambda sl_, par, h0, h1: lambda g: g.dma_start(
                                         out=U2[:, par, :, h0 * P:h1 * P],
                                         in_=usc[sl_][:, :, h0 * P:h1 * P]))(slot, par, h0, h1),
                                     inc="sw")
                                R.op("gpsimd",
                                     (lambda sl_, par, h0, h1: lambda g: g.dma_start(
                                         out=D2[:, par, h0:h1, :],
                                         in_=dsc[sl_][:, h0:h1, :]))(slot, par, h0, h1),
                                     inc="sw")
                            w_half.append((h1, R.cnt["sw"]))
                        w_ready = R.cnt["sw"]
                        w_ready_prev = w_ready
                        if s == 0:
                            R.wait("sync", "sw", w_ready)
                            R.op("sync",
                                 (lambda sl_, par, nfc: lambda g: g.dma_start(
                                     out=gsc[sl_][:, :, 0:nfc * P],
                                     in_=G2[:, par, :, 0:nfc * P]))(slot, par, nfc),
                                 inc="sb")
                            R.op("sync",
                                 (lambda sl_, par, nfc: lambda g: g.dma_start(
                                     out=usc[sl_][:, :, 0:nfc * P],
                                     in_=U2[:, par, :, 0:nfc * P]))(slot, par, nfc),
                                 inc="sb")
                            R.op("sync",
                                 (lambda sl_, par, nfc: lambda g: g.dma_start(
                                     out=dsc[sl_][:, 0:nfc, :],
                                     in_=D2[:, par, 0:nfc, :]))(slot, par, nfc),
                                 inc="sb")
                            wb_done[slot] = R.cnt["sb"]
                        first_slot_of_seq = (slot == 0)

                        def w_wait_for_fc(fc):
                            for (h1, cnt) in w_half:
                                if fc < h1:
                                    return cnt
                            return w_ready
                        if first_ffn:
                            R.wait("tensor", "sdv", c_dv_end)
                            R.wait("tensor", "sac", c_ac_end)
                            first_ffn = False
                        h_ready = {}
                        for g in range(NG):
                            gpar = g % 2
                            for fc in range(nfc):
                                fpar = fc % 2
                                pg = psA[:, fpar * 512:(fpar + 1) * 512]
                                pu = psA[:, 1024 + fpar * 512:1024 + (fpar + 1) * 512]
                                R.wait("tensor", "sw", w_wait_for_fc(fc))
                                if len(hq) >= 2:
                                    R.wait("tensor", "sdv", hq[-2])
                                elif first_slot_of_seq:
                                    BA.write_wait("tensor", fpar * 512, (fpar + 1) * 512)
                                for c in range(NDC):
                                    R.op("tensor",
                                         (lambda c, fc, g, par, pg: lambda t: t.matmul(
                                             pg, G2[:, par, c, fc * P:(fc + 1) * P],
                                             XNT[:, c, g * G:(g + 1) * G],
                                             start=(c == 0), stop=(c == NDC - 1)))(c, fc, g, par, pg),
                                         inc="spe" if c == NDC - 1 else None)
                                g_pe = R.cnt["spe"]
                                if len(hq) < 2 and first_slot_of_seq:
                                    BA.write_wait("tensor", 1024 + fpar * 512, 1024 + (fpar + 1) * 512)
                                for c in range(NDC):
                                    R.op("tensor",
                                         (lambda c, fc, g, par, pu: lambda t: t.matmul(
                                             pu, U2[:, par, c, fc * P:(fc + 1) * P],
                                             XNT[:, c, g * G:(g + 1) * G],
                                             start=(c == 0), stop=(c == NDC - 1)))(c, fc, g, par, pu),
                                         inc="spe" if c == NDC - 1 else None)
                                u_pe = R.cnt["spe"]
                                R.wait("scalar", "spe", g_pe)
                                R.wait("scalar", "sdv", sil_rel[fpar])
                                R.op("scalar",
                                     (lambda fpar, pg: lambda a: a.activation(
                                         SIL[:, fpar, :], pg, AF.Silu))(fpar, pg),
                                     inc="sac")
                                sil_ac = R.cnt["sac"]
                                BA.reader(fpar * 512, (fpar + 1) * 512, "sac", sil_ac)
                                R.wait("vector", "sac", sil_ac)
                                R.wait("vector", "spe", u_pe)
                                R.op("vector",
                                     (lambda g, fc, fpar, pu: lambda v: v.tensor_tensor(
                                         out=H2[:, fc, :], in0=SIL[:, fpar, :],
                                         in1=pu, op=ALU.mult))(g, fc, fpar, pu),
                                     inc="sdv")
                                h_ready[(g, fc)] = R.cnt["sdv"]
                                hq.append(R.cnt["sdv"])
                                sil_rel[fpar] = R.cnt["sdv"]
                                BA.reader(1024 + fpar * 512, 1024 + (fpar + 1) * 512,
                                          "sdv", R.cnt["sdv"])
                            for j in range(4):
                                tt = g * 4 + j
                                dpar = tt % 2
                                pd = psB[:, dpar * D:(dpar + 1) * D]
                                BB.write_wait("tensor", dpar * D, (dpar + 1) * D)
                                for fc in range(nfc):
                                    if fc == 0:
                                        R.wait("tensor", "sdv", h_ready[(g, min(3, nfc - 1))])
                                    elif fc == 4:
                                        R.wait("tensor", "sdv", h_ready[(g, nfc - 1)])
                                    for h in range(2):
                                        R.op("tensor",
                                             (lambda fc, h, j, par, pd, nfc: lambda t: t.matmul(
                                                 pd[:, h * 512:(h + 1) * 512],
                                                 H2[:, fc, j * P:(j + 1) * P],
                                                 D2[:, par, fc, h * 512:(h + 1) * 512],
                                                 start=(fc == 0), stop=(fc == nfc - 1)))(fc, h, j, par, pd, nfc),
                                             inc="spe" if (fc == nfc - 1 and h == 1) else None)
                                d_pe = R.cnt["spe"]
                                R.wait("vector", "spe", d_pe)
                                R.op("vector",
                                     (lambda tt, e, pd: lambda v: v.scalar_tensor_tensor(
                                         out=OA[:, tt, :], in0=pd,
                                         scalar=EW[:, tt, e:e + 1], in1=OA[:, tt, :],
                                         op0=ALU.mult, op1=ALU.add))(tt, e, pd),
                                     inc="sdv")
                                BB.reader(dpar * D, (dpar + 1) * D, "sdv", R.cnt["sdv"])
                                if e == NEXP - 1 and fs0 == FS[-1][0] and j == 3:
                                    R.wait("sync", "sdv")
                                    R.op("sync",
                                         (lambda s, g: lambda gg: gg.dma_start(
                                             out=y[s].rearrange("(tt p) d -> p tt d", p=P)
                                             [:, g * 4:(g + 1) * 4, :],
                                             in_=OA[:, g * 4:(g + 1) * 4, :]))(s, g),
                                         inc="sy")
                                    y_group_done[g] = R.cnt["sy"]
                        if DBG and s == 0 and e == 0 and fs0 == 0:
                            R.wait("sync", "sdv")
                            R.op("sync", lambda g: g.dma_start(out=d_oaf, in_=OA[:]), inc="sy")
                            R.op("sync", lambda g: g.dma_start(out=d_h2, in_=H2[:]), inc="sy")
                            y_done = R.cnt["sy"]
                            R.wait("gpsimd", "sy", y_done)
                        gud_release[slot] = R.cnt["spe"]
                        slot += 1
                ffn_pe_end = R.cnt["spe"]
                ffn_dv_end = R.cnt["sdv"]
                ffn_ac_end = R.cnt["sac"]
                if s == 0:
                    wb_total = R.cnt["sb"]

            y_done = R.cnt["sy"]
            y_gd = dict(y_group_done)

        R.wait("sync", "sy", y_done)

        with nc.Block() as block:
            R.emit(block, sems)

    return nc


_NC_CACHE = None


def kernel(L0_attn_x, L0_attn_Wq, L0_attn_bq, L0_attn_Wk, L0_attn_bk,
           L0_attn_Wv, L0_attn_bv, L0_ffn_router_w, L0_ffn_gate_proj,
           L0_ffn_up_proj, L0_ffn_down_proj):
    global _NC_CACHE
    x = np.ascontiguousarray(np.asarray(L0_attn_x, dtype=np.float32))
    wq = np.asarray(L0_attn_Wq, dtype=np.float32)
    wk = np.asarray(L0_attn_Wk, dtype=np.float32)
    wv = np.asarray(L0_attn_Wv, dtype=np.float32)
    bq = np.asarray(L0_attn_bq, dtype=np.float32)
    bk = np.asarray(L0_attn_bk, dtype=np.float32)
    bv = np.asarray(L0_attn_bv, dtype=np.float32)
    rww = np.asarray(L0_ffn_router_w, dtype=np.float32)
    gp = np.ascontiguousarray(np.asarray(L0_ffn_gate_proj, dtype=np.float32))
    upw = np.ascontiguousarray(np.asarray(L0_ffn_up_proj, dtype=np.float32))
    dp = np.ascontiguousarray(np.asarray(L0_ffn_down_proj, dtype=np.float32))

    wqT = np.ascontiguousarray(wq.T)
    wkT = np.ascontiguousarray(wk.T)
    wvT = np.ascontiguousarray(wv.T)
    bqc = np.ascontiguousarray(bq.reshape(NDC, P).T)
    bkc = np.ascontiguousarray(bk.reshape(NDC, P).T)
    bvr = np.ascontiguousarray(bv.reshape(1, D))
    rwc = np.ascontiguousarray(rww.reshape(NDC, P, NEXP).transpose(1, 0, 2))
    ident = np.eye(P, dtype=np.float32)
    maskd = np.triu(np.full((P, P), -1e9, dtype=np.float32), k=1)
    onesd = np.ones((1, P), dtype=np.float32)

    if _NC_CACHE is None:
        _NC_CACHE = build_program()
    nc = _NC_CACHE

    shared = {"wqT": wqT, "wkT": wkT, "wvT": wvT, "bqc": bqc, "bkc": bkc,
              "bvr": bvr, "rw": rwc, "gate": gp, "up": upw, "down": dp,
              "ident": ident, "maskd": maskd, "onesd": onesd}
    in_maps = []
    for c in range(NCORES):
        m = dict(shared)
        m["x"] = x[c * NSEQ:(c + 1) * NSEQ]
        in_maps.append(m)
    res = run_bass_kernel_spmd(nc, in_maps, core_ids=list(range(NCORES)))
    out = np.concatenate([r["y"] for r in res.results], axis=0)
    return out


# version 15
# speedup vs baseline: 1.0591x; 1.0591x over previous
import numpy as np
from contextlib import ExitStack

import concourse.bass as bass
import concourse.mybir as mybir
from concourse.bass_utils import run_bass_kernel_spmd

F32 = mybir.dt.float32
BF16 = mybir.dt.bfloat16
AF = mybir.ActivationFunctionType
ALU = mybir.AluOpType
AX = mybir.AxisListType

P = 128
NSEQ = 4
T = 2048
NTT = T // P
D = 1024
NDC = D // P
F = 2816
NFC = F // P
NEXP = 8
NG = 4
G = 512
FS = [(0, 8), (8, 8), (16, 6)]
NCORES = 8

ENGINES = ("tensor", "vector", "scalar", "sync", "gpsimd")


class Rec:

    def __init__(self):
        self.items = {e: [] for e in ENGINES}
        self.cnt = {}
        self.step = {}

    def sem(self, name, step=1):
        self.cnt[name] = 0
        self.step[name] = step

    def op(self, eng, fn, inc=None, n=1):
        if inc is not None:
            self.cnt[inc] += self.step[inc] * n
        self.items[eng].append(("op", fn, inc, self.step[inc] * n if inc else 0))
        return self.cnt[inc] if inc else None

    def wait(self, eng, sname, val=None):
        v = self.cnt[sname] if val is None else val
        if v > 0:
            self.items[eng].append(("wait", sname, v))

    def emit(self, block, sems):
        hw = {e: {s: 0 for s in self.cnt} for e in ENGINES}

        def body(eng_name):
            def run(E):
                for it in self.items[eng_name]:
                    if it[0] == "wait":
                        _, s, v = it
                        if v > hw[eng_name][s]:
                            E.wait_ge(sems[s], v)
                            hw[eng_name][s] = v
                    else:
                        _, fn, inc, k = it
                        instr = fn(E)
                        if inc is not None:
                            instr.then_inc(sems[inc], k)
            return run

        block.tensor(body("tensor"))
        block.vector(body("vector"))
        block.scalar(body("scalar"))
        block.sync(body("sync"))
        block.gpsimd(body("gpsimd"))


class Banks:

    def __init__(self, R):
        self.R = R
        self.rel = [{"sdv": 0, "sac": 0} for _ in range(4)]

    def _banks(self, c0, c1):
        return range(c0 // 512, (c1 + 511) // 512)

    def reader(self, c0, c1, sem, val):
        for b in self._banks(c0, c1):
            self.rel[b][sem] = max(self.rel[b][sem], val)

    def write_wait(self, eng, c0, c1):
        for b in self._banks(c0, c1):
            for sem in ("sdv", "sac"):
                self.R.wait(eng, sem, self.rel[b][sem])


def build_program(DBG=False):
    nc = bass.Bass("TRN2", target_bir_lowering=False, debug=False)
    x = nc.dram_tensor("x", [NSEQ, T, D], F32, kind="ExternalInput").ap()
    wqT = nc.dram_tensor("wqT", [D, D], F32, kind="ExternalInput").ap()
    wkT = nc.dram_tensor("wkT", [D, D], F32, kind="ExternalInput").ap()
    wvT = nc.dram_tensor("wvT", [D, D], F32, kind="ExternalInput").ap()
    bqc = nc.dram_tensor("bqc", [P, NDC], F32, kind="ExternalInput").ap()
    bkc = nc.dram_tensor("bkc", [P, NDC], F32, kind="ExternalInput").ap()
    bvr = nc.dram_tensor("bvr", [1, D], F32, kind="ExternalInput").ap()
    rw = nc.dram_tensor("rw", [P, NDC, NEXP], F32, kind="ExternalInput").ap()
    gate = nc.dram_tensor("gate", [NEXP, D, F], F32, kind="ExternalInput").ap()
    up = nc.dram_tensor("up", [NEXP, D, F], F32, kind="ExternalInput").ap()
    down = nc.dram_tensor("down", [NEXP, F, D], F32, kind="ExternalInput").ap()
    ident = nc.dram_tensor("ident", [P, P], F32, kind="ExternalInput").ap()
    maskd = nc.dram_tensor("maskd", [P, P], F32, kind="ExternalInput").ap()
    onesd = nc.dram_tensor("onesd", [1, P], F32, kind="ExternalInput").ap()
    y = nc.dram_tensor("y", [NSEQ, T, D], F32, kind="ExternalOutput").ap()
    NSLOT = NEXP * len(FS)
    gsc = nc.dram_tensor("gsc", [NSLOT, P, NDC, 1024], BF16, kind="Internal").ap()
    usc = nc.dram_tensor("usc", [NSLOT, P, NDC, 1024], BF16, kind="Internal").ap()
    dsc = nc.dram_tensor("dsc", [NSLOT, P, NDC, 1024], BF16, kind="Internal").ap()
    if DBG:
        d_xnt = nc.dram_tensor("d_xnt", [P, NDC, T], BF16, kind="ExternalOutput").ap()
        d_stats = nc.dram_tensor("d_stats", [P, 3, NTT], F32, kind="ExternalOutput").ap()
        d_ew = nc.dram_tensor("d_ew", [P, NTT, NEXP], F32, kind="ExternalOutput").ap()
        d_kt = nc.dram_tensor("d_kt", [P, NDC, T], BF16, kind="ExternalOutput").ap()
        d_vt = nc.dram_tensor("d_vt", [P, NTT, D], BF16, kind="ExternalOutput").ap()
        d_qg = nc.dram_tensor("d_qg", [P, NDC, G], BF16, kind="ExternalOutput").ap()
        d_pb = nc.dram_tensor("d_pb", [P, T], BF16, kind="ExternalOutput").ap()
        d_smx = nc.dram_tensor("d_smx", [P, 2, 4], F32, kind="ExternalOutput").ap()
        d_oa = nc.dram_tensor("d_oa", [P, NTT, D], F32, kind="ExternalOutput").ap()
        d_oaf = nc.dram_tensor("d_oaf", [P, NTT, D], F32, kind="ExternalOutput").ap()
        d_h2 = nc.dram_tensor("d_h2", [P, NDC, G], BF16, kind="ExternalOutput").ap()
        d_x = nc.dram_tensor("d_x", [P, NTT, D], F32, kind="ExternalOutput").ap()

    R = Rec()
    for sname in ("sx", "sy", "sw", "sl", "sb"):
        R.sem(sname, 16)
    for sname in ("spe", "sdv", "sac"):
        R.sem(sname, 1)

    with ExitStack() as ctx:
        sb = lambda name, shape, dt: ctx.enter_context(nc.sbuf_tensor(name, shape, dt))
        OA = sb("OA", [P, NTT, D], F32)
        XNT = sb("XNT", [P, NDC, T], BF16)
        sum16 = sb("sum16", [P, NTT], F32)
        ss16 = sb("ss16", [P, NTT], F32)
        mean16 = sb("mean16", [P, NTT], F32)
        var16 = sb("var16", [P, NTT], F32)
        rstd16 = sb("rstd16", [P, NTT], F32)
        nmr16 = sb("nmr16", [P, NTT], F32)
        tmp16 = sb("tmp16", [P, NTT], F32)
        EW = sb("EW", [P, NTT, NEXP], F32)
        EQ1 = sb("EQ1", [P, NTT, NEXP], BF16)
        EQ2 = sb("EQ2", [P, NTT, NEXP], BF16)
        DLT = sb("DLT", [P, NTT], F32)
        W1 = sb("W1", [P, NTT], F32)
        W2 = sb("W2", [P, NTT], F32)
        lgt = sb("lgt", [P, NEXP], F32)
        l2b = sb("l2b", [P, NEXP], F32)
        m1b = sb("m1b", [P, 1], F32)
        m2b = sb("m2b", [P, 1], F32)
        smx = sb("smx", [P, 2, 4], F32)
        nmx = sb("nmx", [P, 2, 4], F32)
        scn = sb("scn", [P, 2, 4], F32)
        identT = sb("identT", [P, P], BF16)
        maskT = sb("maskT", [P, P], BF16)
        RW = sb("RW", [P, NDC, NEXP], BF16)
        bqs = sb("bqs", [P, NDC], F32)
        bks = sb("bks", [P, NDC], F32)
        WINB = 108544
        arena = ctx.enter_context(nc.sbuf_tensor("win_arena", [P, WINB], mybir.dt.uint8))
        abase = nc.lookup_mloc(arena).addr
        at = lambda name, shape, dt, off: nc.alloc_sbuf_tensor_at(
            name, shape, dt, offset=abase + off)
        KT = at("KT", [P, NDC, T], BF16, 0)
        W8b = at("W8b", [P, NDC, D], BF16, 81920)
        VT = at("VT", [P, NTT, D], BF16, 32768)
        W8 = at("W8", [P, NDC, D], BF16, 65536)
        QG = at("QG", [P, NDC, G], BF16, 81920)
        Pb = at("Pb", [P, T], BF16, 90112)
        PT = at("PT", [P, NTT, P], BF16, 94208)
        G2 = at("G2", [P, 2, NDC, 1024], BF16, 0)
        U2 = at("U2", [P, 2, NDC, 1024], BF16, 32768)
        D2 = at("D2", [P, 2, NDC, 1024], BF16, 65536)
        H2 = at("H2", [P, NDC, G], BF16, 98304)
        SIL = at("SIL", [P, 2, 512], BF16, 106496)
        xn_tok = at("xn_tok", [P, 2, D], BF16, 0)
        dumpA = at("dumpA", [P, D], BF16, 4096)
        bvs = at("bvs", [1, D], BF16, 98304)
        ones1 = at("ones1", [1, P], BF16, 100352)
        psA = ctx.enter_context(nc.psum_tensor("psA", [P, 2048], F32))
        psB = ctx.enter_context(nc.psum_tensor("psB", [P, 2048], F32))
        psAb = psA.bitcast(BF16)

        sems = {}
        for sname in R.cnt:
            sems[sname] = ctx.enter_context(nc.semaphore(sname))

        BA = Banks(R)
        BB = Banks(R)

        R.op("gpsimd", lambda g: g.dma_start(out=identT[:], in_=ident), inc="sl")
        R.op("gpsimd", lambda g: g.dma_start(out=RW[:], in_=rw), inc="sl")
        R.op("gpsimd", lambda g: g.dma_start(out=maskT[:], in_=maskd), inc="sl")
        R.op("sync", lambda g: g.dma_start(out=bqs[:], in_=bqc), inc="sl")
        R.op("sync", lambda g: g.dma_start(out=bks[:], in_=bkc), inc="sl")
        consts_done = R.cnt["sl"]
        for e in ("tensor", "vector", "scalar"):
            R.wait(e, "sl", consts_done)

        ffn_pe_end = 0
        ffn_dv_end = 0
        ffn_ac_end = 0
        y_done = 0
        y_gd = {}
        wb_done = {}
        wb_total = 0

        for s in range(NSEQ):
            xgload = []
            for g in range(4):
                if s > 0:
                    R.wait("sync", "sy", y_gd.get(g, y_done))
                R.op("sync",
                     (lambda s, g: lambda gg: gg.dma_start(
                         out=OA[:, g * 4:(g + 1) * 4, :],
                         in_=x[s].rearrange("(tt p) d -> p tt d", p=P)
                         [:, g * 4:(g + 1) * 4, :]))(s, g),
                     inc="sx")
                xgload.append(R.cnt["sx"])
            xload = [xgload[tt // 4] for tt in range(NTT)]

            grp_nmr_dv = {}
            if DBG and s == 0:
                R.wait("sync", "sx", xload[-1])
                R.op("sync", lambda g: g.dma_start(out=d_x, in_=OA[:]), inc="sy")
            xnt_done = []
            rlogs = [psB[:, 1024:1024 + NEXP], psB[:, 1536:1536 + NEXP]]
            tr_pe_prev = 0
            rt_pe_by_tt = {}

            def top2_chain(tt):
                rlog = rlogs[tt % 2]
                c0 = 1024 + (tt % 2) * 512
                R.wait("vector", "spe", rt_pe_by_tt[tt])
                R.op("vector", (lambda rlog: lambda v: v.tensor_copy(lgt[:], rlog))(rlog),
                     inc="sdv")
                BB.reader(c0, c0 + NEXP, "sdv", R.cnt["sdv"])
                R.wait("vector", "sdv")
                R.op("vector", lambda v: v.reduce_max(m1b[:], lgt[:], axis=AX.X), inc="sdv")
                R.wait("vector", "sdv")
                R.op("vector",
                     (lambda tt: lambda v: v.tensor_scalar(
                         out=EQ1[:, tt, :], in0=lgt[:], scalar1=m1b[:], scalar2=None,
                         op0=ALU.is_equal, op1=ALU.bypass))(tt), inc="sdv")
                R.wait("vector", "sdv")
                R.op("vector",
                     (lambda tt: lambda v: v.scalar_tensor_tensor(
                         out=l2b[:], in0=EQ1[:, tt, :], scalar=-1e9, in1=lgt[:],
                         op0=ALU.mult, op1=ALU.add))(tt), inc="sdv")
                R.wait("vector", "sdv")
                R.op("vector", lambda v: v.reduce_max(m2b[:], l2b[:], axis=AX.X), inc="sdv")
                R.wait("vector", "sdv")
                R.op("vector",
                     (lambda tt: lambda v: v.tensor_scalar(
                         out=EQ2[:, tt, :], in0=l2b[:], scalar1=m2b[:], scalar2=None,
                         op0=ALU.is_equal, op1=ALU.bypass))(tt), inc="sdv")
                R.op("vector",
                     (lambda tt: lambda v: v.tensor_tensor(
                         out=DLT[:, tt:tt + 1], in0=m2b[:], in1=m1b[:],
                         op=ALU.subtract))(tt), inc="sdv")

            def a_tile_pipeline(tt):
                nonlocal tr_pe_prev
                par = tt % 2
                R.wait("scalar", "sdv", grp_nmr_dv[tt // 4])
                if tt >= 2:
                    R.wait("scalar", "spe", tr_pe_prev)
                R.op("scalar",
                     (lambda tt, par: lambda a: a.activation(
                         xn_tok[:, par, :], OA[:, tt, :], AF.Identity,
                         scale=rstd16[:, tt:tt + 1],
                         bias=nmr16[:, tt:tt + 1]))(tt, par),
                     inc="sac")
                xn_dv = R.cnt["sac"]
                R.wait("tensor", "sac", xn_dv)
                BA.write_wait("tensor", 0, 512)
                for c in range(NDC):
                    R.op("tensor",
                         (lambda c, par: lambda t: t.transpose(
                             psAb[:, c * P:(c + 1) * P],
                             xn_tok[:, par, c * P:(c + 1) * P], identT[:]))(c, par),
                         inc="spe" if c == NDC - 1 else None)
                tr_pe_prev = R.cnt["spe"]
                R.wait("vector", "spe", tr_pe_prev)
                R.op("vector",
                     (lambda tt: lambda v: v.tensor_copy(
                         XNT[:, :, tt * P:(tt + 1) * P],
                         psAb[:, 0:NDC * P].rearrange("p (c n) -> p c n", c=NDC)))(tt),
                     inc="sdv")
                xnt_done.append(("sdv", R.cnt["sdv"]))
                BA.reader(0, 512, "sdv", R.cnt["sdv"])
                R.wait("tensor", xnt_done[-1][0], xnt_done[-1][1])
                c0 = 1024 + (tt % 2) * 512
                BB.write_wait("tensor", c0, c0 + NEXP)
                for c in range(NDC):
                    R.op("tensor",
                         (lambda c, tt, rlog: lambda t: t.matmul(
                             rlog, XNT[:, c, tt * P:(tt + 1) * P], RW[:, c, :],
                             start=(c == 0), stop=(c == NDC - 1)))(c, tt, rlogs[tt % 2]),
                         inc="spe" if c == NDC - 1 else None)
                rt_pe_by_tt[tt] = R.cnt["spe"]
                if tt >= 1:
                    top2_chain(tt - 1)

            for gA in range(4):
                g0 = gA * 4
                for tt in range(g0, g0 + 4):
                    dumpv = XNT[:, :, tt * P:(tt + 1) * P]
                    R.wait("vector", "sx", xgload[gA])
                    R.op("vector",
                         (lambda tt, dumpv: lambda v: v.scalar_tensor_tensor(
                             out=dumpv, in0=OA[:, tt, :], scalar=1.0, in1=OA[:, tt, :],
                             op0=ALU.bypass, op1=ALU.mult,
                             accum_out=ss16[:, tt:tt + 1]))(tt, dumpv),
                         inc="sdv")
                    R.wait("scalar", "sx", xgload[gA])
                    R.op("scalar",
                         (lambda tt: lambda a: a.activation(
                             dumpA[:], OA[:, tt, :], AF.Identity,
                             accum_out=sum16[:, tt:tt + 1]))(tt),
                         inc="sac")
                sl4 = slice(g0, g0 + 4)
                R.wait("vector", "sdv")
                R.wait("vector", "sac")
                R.op("vector", (lambda sl4: lambda v: v.tensor_scalar(
                    out=mean16[:, sl4], in0=sum16[:, sl4], scalar1=1.0 / D, scalar2=None,
                    op0=ALU.mult, op1=ALU.bypass))(sl4), inc="sdv")
                R.wait("vector", "sdv")
                R.op("vector", (lambda sl4: lambda v: v.scalar_tensor_tensor(
                    out=tmp16[:, sl4], in0=mean16[:, sl4], scalar=1.0, in1=mean16[:, sl4],
                    op0=ALU.bypass, op1=ALU.mult))(sl4), inc="sdv")
                R.wait("vector", "sdv")
                R.op("vector", (lambda sl4: lambda v: v.scalar_tensor_tensor(
                    out=var16[:, sl4], in0=ss16[:, sl4], scalar=1.0 / D, in1=tmp16[:, sl4],
                    op0=ALU.mult, op1=ALU.subtract))(sl4), inc="sdv")
                R.wait("vector", "sdv")
                R.op("vector", (lambda sl4: lambda v: v.tensor_scalar(
                    out=var16[:, sl4], in0=var16[:, sl4], scalar1=1e-5, scalar2=None,
                    op0=ALU.add, op1=ALU.bypass))(sl4), inc="sdv")
                R.wait("scalar", "sdv")
                R.op("scalar", (lambda sl4: lambda a: a.activation(
                    tmp16[:, sl4], var16[:, sl4], AF.Sqrt))(sl4), inc="sac")
                R.wait("vector", "sac")
                R.op("vector", (lambda sl4: lambda v: v.reciprocal(
                    rstd16[:, sl4], tmp16[:, sl4]))(sl4), inc="sdv")
                R.wait("vector", "sdv")
                R.op("vector", (lambda sl4: lambda v: v.scalar_tensor_tensor(
                    out=nmr16[:, sl4], in0=mean16[:, sl4], scalar=-1.0, in1=rstd16[:, sl4],
                    op0=ALU.mult, op1=ALU.mult))(sl4), inc="sdv")
                grp_nmr_dv[gA] = R.cnt["sdv"]
                for tt in range(g0, g0 + 4):
                    a_tile_pipeline(tt)
            top2_chain(NTT - 1)
            R.wait("scalar", "sdv")
            R.op("scalar", lambda a: a.activation(W1[:], DLT[:], AF.Exp), inc="sac")
            R.wait("vector", "sac")
            R.op("vector", lambda v: v.tensor_scalar(
                out=W1[:], in0=W1[:], scalar1=1.0, scalar2=None,
                op0=ALU.add, op1=ALU.bypass), inc="sdv")
            R.wait("vector", "sdv")
            R.op("vector", lambda v: v.reciprocal(W1[:], W1[:]), inc="sdv")
            R.wait("vector", "sdv")
            R.op("vector", lambda v: v.tensor_scalar(
                out=W2[:], in0=W1[:], scalar1=-1.0, scalar2=1.0,
                op0=ALU.mult, op1=ALU.add), inc="sdv")
            R.wait("vector", "sdv")
            for tt in range(NTT):
                R.op("vector",
                     (lambda tt: lambda v: v.tensor_scalar(
                         out=EW[:, tt, :], in0=EQ1[:, tt, :], scalar1=W1[:, tt:tt + 1],
                         scalar2=None, op0=ALU.mult, op1=ALU.bypass))(tt), inc="sdv")
                R.wait("vector", "sdv")
                R.op("vector",
                     (lambda tt: lambda v: v.scalar_tensor_tensor(
                         out=EW[:, tt, :], in0=EQ2[:, tt, :], scalar=W2[:, tt:tt + 1],
                         in1=EW[:, tt, :], op0=ALU.mult, op1=ALU.add))(tt), inc="sdv")
                R.wait("vector", "sdv")
            a_end_dv = R.cnt["sdv"]
            if DBG and s == 0:
                R.wait("sync", "sdv", a_end_dv)
                R.op("sync", lambda g: g.dma_start(out=d_xnt, in_=XNT[:]), inc="sy")
                R.op("sync", lambda g: g.dma_start(out=d_stats[:, 0, :], in_=mean16[:]), inc="sy")
                R.op("sync", lambda g: g.dma_start(out=d_stats[:, 1, :], in_=rstd16[:]), inc="sy")
                R.op("sync", lambda g: g.dma_start(out=d_stats[:, 2, :], in_=var16[:]), inc="sy")
                R.op("sync", lambda g: g.dma_start(out=d_ew, in_=EW[:]), inc="sy")
                y_done = R.cnt["sy"]

            if True:
                R.wait("gpsimd", "spe", ffn_pe_end)
                R.wait("gpsimd", "sdv", ffn_dv_end)
                R.wait("gpsimd", "sac", ffn_ac_end)
                R.wait("gpsimd", "sl", consts_done)
                if s == 1:
                    R.wait("gpsimd", "sb", wb_total)
                R.op("gpsimd", lambda g: g.dma_start(
                    out=W8[:], in_=wkT.rearrange("(c p) e -> p c e", p=P)), inc="sw")
                R.op("gpsimd", lambda g: g.dma_start(out=bvs[:], in_=bvr), inc="sw")
                R.op("gpsimd", lambda g: g.dma_start(out=ones1[:], in_=onesd), inc="sw")
                R.op("gpsimd", lambda g: g.dma_start(
                    out=W8b[:], in_=wvT.rearrange("(c p) e -> p c e", p=P)), inc="sw")
                wk_ld = R.cnt["sw"]
                wv_ld = wk_ld
                R.wait("tensor", "sw", wk_ld)
                for ec in range(NDC):
                    for kb in range(NG):
                        R.wait("tensor", xnt_done[kb * 4 + 3][0], xnt_done[kb * 4 + 3][1])
                        par = (ec * NG + kb) % 2
                        pb = psB[:, par * 512:(par + 1) * 512]
                        BB.write_wait("tensor", par * 512, (par + 1) * 512)
                        for c in range(NDC):
                            R.op("tensor",
                                 (lambda c, ec, kb, pb: lambda t: t.matmul(
                                     pb, W8[:, c, ec * P:(ec + 1) * P],
                                     XNT[:, c, kb * G:(kb + 1) * G],
                                     start=(c == 0), stop=(c == NDC - 1)))(c, ec, kb, pb),
                                 inc="spe" if c == NDC - 1 else None)
                        R.wait("scalar", "spe")
                        R.op("scalar",
                             (lambda ec, kb, pb: lambda a: a.activation(
                                 KT[:, ec, kb * G:(kb + 1) * G], pb, AF.Identity,
                                 bias=bks[:, ec:ec + 1]))(ec, kb, pb), inc="sac")
                        BB.reader(par * 512, (par + 1) * 512, "sac", R.cnt["sac"])
                k_pe_end = R.cnt["spe"]
                R.wait("gpsimd", "spe", k_pe_end)
                R.op("gpsimd", lambda g: g.dma_start(
                    out=W8[:], in_=wqT.rearrange("(c p) e -> p c e", p=P)), inc="sw")
                wq_ld = R.cnt["sw"]

                R.wait("tensor", "sw", wv_ld)
                for tt in range(NTT):
                    par = tt % 2
                    pv = psB[:, par * D:(par + 1) * D]
                    BB.write_wait("tensor", par * D, (par + 1) * D)
                    for h in range(2):
                        for c in range(NDC):
                            R.op("tensor",
                                 (lambda c, tt, h, pv: lambda t: t.matmul(
                                     pv[:, h * 512:(h + 1) * 512],
                                     XNT[:, c, tt * P:(tt + 1) * P],
                                     W8b[:, c, h * 512:(h + 1) * 512],
                                     start=(c == 0), stop=False))(c, tt, h, pv),
                                 inc=None)
                        R.op("tensor",
                             (lambda tt, h, pv: lambda t: t.matmul(
                                 pv[:, h * 512:(h + 1) * 512], ones1[:],
                                 bvs[:, h * 512:(h + 1) * 512],
                                 start=False, stop=True))(tt, h, pv),
                             inc="spe" if h == 1 else None)
                    R.wait("vector", "spe")
                    R.op("vector",
                         (lambda tt, pv: lambda v: v.tensor_copy(VT[:, tt, :], pv))(tt, pv),
                         inc="sdv")
                    BB.reader(par * D, (par + 1) * D, "sdv", R.cnt["sdv"])
                v_pe_end = R.cnt["spe"]
                if DBG and s == 0:
                    R.wait("sync", "sac")
                    R.wait("sync", "sdv")
                    R.op("sync", lambda g: g.dma_start(out=d_kt, in_=KT[:]), inc="sy")
                    R.op("sync", lambda g: g.dma_start(out=d_vt, in_=VT[:]), inc="sy")
                    y_done = R.cnt["sy"]

                R.wait("tensor", "sw", wq_ld)
                for g in range(NG):
                    for ec in range(NDC):
                        par = ec % 2
                        pq = psA[:, par * 512:(par + 1) * 512]
                        BA.write_wait("tensor", par * 512, (par + 1) * 512)
                        for c in range(NDC):
                            R.op("tensor",
                                 (lambda c, ec, g, pq: lambda t: t.matmul(
                                     pq, W8[:, c, ec * P:(ec + 1) * P],
                                     XNT[:, c, g * G:(g + 1) * G],
                                     start=(c == 0), stop=(c == NDC - 1)))(c, ec, g, pq),
                                 inc="spe" if c == NDC - 1 else None)
                        R.wait("scalar", "spe")
                        R.op("scalar",
                             (lambda ec, pq: lambda a: a.activation(
                                 QG[:, ec, :], pq, AF.Identity,
                                 bias=bqs[:, ec:ec + 1]))(ec, pq), inc="sac")
                        BA.reader(par * 512, (par + 1) * 512, "sac", R.cnt["sac"])
                    q_ready = R.cnt["sac"]

                    for j in range(4):
                        Tq = g * 4 + j
                        klen = (Tq + 1) * P
                        nb = (klen + 511) // 512
                        par = Tq % 2
                        R.wait("tensor", "sac", q_ready)
                        BA.write_wait("tensor", 0, klen)
                        blk_pe = []
                        for kb in range(nb):
                            ncols = min(512, klen - kb * 512)
                            is_diag = (kb == nb - 1)
                            for c in range(NDC):
                                R.op("tensor",
                                     (lambda c, kb, ncols, g, j, is_diag: lambda t: t.matmul(
                                         psA[:, kb * 512:kb * 512 + ncols],
                                         QG[:, c, j * P:(j + 1) * P],
                                         KT[:, c, kb * 512:kb * 512 + ncols],
                                         start=(c == 0),
                                         stop=(c == NDC - 1 and not is_diag)))(c, kb, ncols, g, j, is_diag),
                                     inc="spe" if (c == NDC - 1 and not is_diag) else None)
                            if is_diag:
                                R.op("tensor",
                                     (lambda Tq, kb, ncols: lambda t: t.matmul(
                                         psA[:, kb * 512 + ncols - P:kb * 512 + ncols],
                                         identT[:], maskT[:],
                                         start=False, stop=True))(Tq, kb, ncols),
                                     inc="spe")
                            blk_pe.append(R.cnt["spe"])
                        s_pe = R.cnt["spe"]
                        for kb in range(nb):
                            ncols = min(512, klen - kb * 512)
                            R.wait("vector", "spe", blk_pe[kb])
                            R.op("vector",
                                 (lambda kb, ncols, par: lambda v: v.reduce_max(
                                     nmx[:, par, kb:kb + 1],
                                     psA[:, kb * 512:kb * 512 + ncols], axis=AX.X,
                                     negate=True))(kb, ncols, par), inc="sdv")
                            BA.reader(kb * 512, kb * 512 + ncols, "sdv", R.cnt["sdv"])
                        R.wait("vector", "sdv")
                        R.op("vector",
                             (lambda nb, par: lambda v: v.tensor_scalar(
                                 out=scn[:, par, 0:nb], in0=nmx[:, par, 0:nb],
                                 scalar1=1.0 / 32.0, scalar2=None,
                                 op0=ALU.mult, op1=ALU.min,
                                 accum_out=smx[:, par, 1:2]))(nb, par), inc="sdv")
                        mx_dv = R.cnt["sdv"]
                        nkb = Tq + 1
                        exp_ac_blk = []
                        R.wait("scalar", "sdv", mx_dv)
                        for kb in range(nb):
                            ncols = min(512, klen - kb * 512)
                            R.op("scalar",
                                 (lambda kb, ncols, par: lambda a: a.activation(
                                     Pb[:, kb * 512:kb * 512 + ncols],
                                     psA[:, kb * 512:kb * 512 + ncols], AF.Exp,
                                     bias=smx[:, par, 1:2], scale=1.0 / 32.0,
                                     accum_out=nmx[:, par, kb:kb + 1]))(kb, ncols, par),
                                 inc="sac")
                            exp_ac_blk.append(R.cnt["sac"])
                            BA.reader(kb * 512, kb * 512 + ncols, "sac", R.cnt["sac"])
                        exp_ac = R.cnt["sac"]
                        R.wait("vector", "sac", exp_ac)
                        R.op("vector",
                             (lambda nb, par: lambda v: v.tensor_scalar(
                                 out=scn[:, par, 0:nb], in0=nmx[:, par, 0:nb],
                                 scalar1=0.0, scalar2=None,
                                 op0=ALU.add, op1=ALU.add,
                                 accum_out=smx[:, par, 2:3]))(nb, par), inc="sdv")
                        R.wait("vector", "sdv")
                        R.op("vector",
                             (lambda par: lambda v: v.reciprocal(
                                 smx[:, par, 3:4], smx[:, par, 2:3]))(par), inc="sdv")
                        recip_dv = R.cnt["sdv"]
                        BA.write_wait("tensor", 0, nkb * P // 2)
                        copy_dv = []
                        for kb in range(nkb):
                            R.wait("tensor", "sac", exp_ac_blk[kb // 4])
                            R.op("tensor",
                                 (lambda kb: lambda t: t.transpose(
                                     psAb[:, kb * P:(kb + 1) * P],
                                     Pb[:, kb * P:(kb + 1) * P], identT[:]))(kb),
                                 inc="spe" if (kb % 8 == 7 or kb == nkb - 1) else None)
                            if kb % 8 == 7 or kb == nkb - 1:
                                b0 = (kb // 8) * 8
                                cntk = kb + 1 - b0
                                R.wait("vector", "spe")
                                R.op("vector",
                                     (lambda b0, cntk: lambda v: v.tensor_copy(
                                         PT[:, b0:b0 + cntk, :],
                                         psAb[:, b0 * P:(b0 + cntk) * P]
                                         .rearrange("p (k n) -> p k n", n=P)))(b0, cntk),
                                     inc="sdv")
                                copy_dv.append(R.cnt["sdv"])
                                BA.reader(b0 * P // 2, (b0 + cntk) * P // 2, "sdv", R.cnt["sdv"])
                        tr_pe = R.cnt["spe"]
                        pav = psB[:, par * D:(par + 1) * D]
                        BB.write_wait("tensor", par * D, (par + 1) * D)
                        for kb in range(nkb):
                            if kb % 8 == 0:
                                R.wait("tensor", "sdv",
                                       copy_dv[min(kb // 8, len(copy_dv) - 1)])
                            for h in range(2):
                                R.op("tensor",
                                     (lambda kb, h, pav, nkb: lambda t: t.matmul(
                                         pav[:, h * 512:(h + 1) * 512], PT[:, kb, :],
                                         VT[:, kb, h * 512:(h + 1) * 512],
                                         start=(kb == 0), stop=(kb == nkb - 1)))(kb, h, pav, nkb),
                                     inc="spe" if (kb == nkb - 1 and h == 1) else None)
                        av_pe = R.cnt["spe"]
                        R.wait("vector", "spe", av_pe)
                        R.wait("vector", "sdv", recip_dv)
                        R.op("vector",
                             (lambda Tq, par, pav: lambda v: v.scalar_tensor_tensor(
                                 out=OA[:, Tq, :], in0=pav, scalar=smx[:, par, 3:4],
                                 in1=OA[:, Tq, :], op0=ALU.mult, op1=ALU.add))(Tq, par, pav),
                             inc="sdv")
                        BB.reader(par * D, (par + 1) * D, "sdv", R.cnt["sdv"])
                c_pe_end = R.cnt["spe"]
                c_dv_end = R.cnt["sdv"]
                c_ac_end = R.cnt["sac"]
                if DBG and s == 0:
                    R.wait("sync", "sdv", c_dv_end)
                    R.wait("sync", "sac", c_ac_end)
                    R.op("sync", lambda g: g.dma_start(out=d_qg, in_=QG[:]), inc="sy")
                    R.op("sync", lambda g: g.dma_start(out=d_pb, in_=Pb[:]), inc="sy")
                    R.op("sync", lambda g: g.dma_start(out=d_smx, in_=smx[:]), inc="sy")
                    R.op("sync", lambda g: g.dma_start(out=d_oa, in_=OA[:]), inc="sy")
                    y_done = R.cnt["sy"]

            if True:
                gud_release = {}
                y_group_done = {}
                hq = []
                sil_rel = {0: 0, 1: 0}
                slot = 0
                first_ffn = True
                w_ready_prev = 0
                for e in range(NEXP):
                    for (fs0, nfc) in FS:
                        par = slot % 2
                        if slot <= 1:
                            R.wait("gpsimd", "spe", c_pe_end)
                            R.wait("gpsimd", "sdv", c_dv_end)
                            R.wait("gpsimd", "sac", c_ac_end)
                            if DBG:
                                R.wait("gpsimd", "sy", y_done)
                        else:
                            R.wait("gpsimd", "spe", gud_release[slot - 2])
                        if s == 0 and slot >= 2:
                            R.wait("gpsimd", "sb", wb_done[slot - 2])
                        if s == 1 and slot <= 1:
                            R.wait("gpsimd", "sb", wb_total)
                        R.wait("gpsimd", "sw", w_ready_prev)
                        halves = ([(0, 2), (2, 4), (4, 6), (6, nfc)]
                                  if slot == 0 else [(0, nfc)])
                        w_half = []
                        for (h0, h1) in halves:
                            if s == 0:
                                R.op("gpsimd",
                                     (lambda e, fs0, par, h0, h1: lambda g: g.dma_start(
                                         out=G2[:, par, :, h0 * P:h1 * P],
                                         in_=gate[e].rearrange("(c p) f -> p c f", p=P)
                                         [:, :, (fs0 + h0) * P:(fs0 + h1) * P]))(e, fs0, par, h0, h1),
                                     inc="sw")
                                R.op("gpsimd",
                                     (lambda e, fs0, par, h0, h1: lambda g: g.dma_start(
                                         out=U2[:, par, :, h0 * P:h1 * P],
                                         in_=up[e].rearrange("(c p) f -> p c f", p=P)
                                         [:, :, (fs0 + h0) * P:(fs0 + h1) * P]))(e, fs0, par, h0, h1),
                                     inc="sw")
                                R.op("gpsimd",
                                     (lambda e, fs0, par, h0, h1: lambda g: g.dma_start(
                                         out=D2[:, par, h0:h1, :],
                                         in_=down[e].rearrange("(c p) d -> p c d", p=P)
                                         [:, fs0 + h0:fs0 + h1, :]))(e, fs0, par, h0, h1),
                                     inc="sw")
                            else:
                                R.op("gpsimd",
                                     (lambda sl_, par, h0, h1: lambda g: g.dma_start(
                                         out=G2[:, par, :, h0 * P:h1 * P],
                                         in_=gsc[sl_][:, :, h0 * P:h1 * P]))(slot, par, h0, h1),
                                     inc="sw")
                                R.op("gpsimd",
                                     (lambda sl_, par, h0, h1: lambda g: g.dma_start(
                                         out=U2[:, par, :, h0 * P:h1 * P],
                                         in_=usc[sl_][:, :, h0 * P:h1 * P]))(slot, par, h0, h1),
                                     inc="sw")
                                R.op("gpsimd",
                                     (lambda sl_, par, h0, h1: lambda g: g.dma_start(
                                         out=D2[:, par, h0:h1, :],
                                         in_=dsc[sl_][:, h0:h1, :]))(slot, par, h0, h1),
                                     inc="sw")
                            w_half.append((h1, R.cnt["sw"]))
                        w_ready = R.cnt["sw"]
                        w_ready_prev = w_ready
                        if s == 0:
                            R.wait("sync", "sw", w_ready)
                            R.op("sync",
                                 (lambda sl_, par, nfc: lambda g: g.dma_start(
                                     out=gsc[sl_][:, :, 0:nfc * P],
                                     in_=G2[:, par, :, 0:nfc * P]))(slot, par, nfc),
                                 inc="sb")
                            R.op("sync",
                                 (lambda sl_, par, nfc: lambda g: g.dma_start(
                                     out=usc[sl_][:, :, 0:nfc * P],
                                     in_=U2[:, par, :, 0:nfc * P]))(slot, par, nfc),
                                 inc="sb")
                            R.op("sync",
                                 (lambda sl_, par, nfc: lambda g: g.dma_start(
                                     out=dsc[sl_][:, 0:nfc, :],
                                     in_=D2[:, par, 0:nfc, :]))(slot, par, nfc),
                                 inc="sb")
                            wb_done[slot] = R.cnt["sb"]
                        first_slot_of_seq = (slot == 0)

                        def w_wait_for_fc(fc):
                            for (h1, cnt) in w_half:
                                if fc < h1:
                                    return cnt
                            return w_ready
                        if first_ffn:
                            R.wait("tensor", "sdv", c_dv_end)
                            R.wait("tensor", "sac", c_ac_end)
                            first_ffn = False
                        h_ready = {}
                        for g in range(NG):
                            gpar = g % 2
                            for fc in range(nfc):
                                fpar = fc % 2
                                pg = psA[:, fpar * 512:(fpar + 1) * 512]
                                pu = psA[:, 1024 + fpar * 512:1024 + (fpar + 1) * 512]
                                R.wait("tensor", "sw", w_wait_for_fc(fc))
                                if len(hq) >= 2:
                                    R.wait("tensor", "sdv", hq[-2])
                                elif first_slot_of_seq:
                                    BA.write_wait("tensor", fpar * 512, (fpar + 1) * 512)
                                for c in range(NDC):
                                    R.op("tensor",
                                         (lambda c, fc, g, par, pg: lambda t: t.matmul(
                                             pg, G2[:, par, c, fc * P:(fc + 1) * P],
                                             XNT[:, c, g * G:(g + 1) * G],
                                             start=(c == 0), stop=(c == NDC - 1)))(c, fc, g, par, pg),
                                         inc="spe" if c == NDC - 1 else None)
                                g_pe = R.cnt["spe"]
                                if len(hq) < 2 and first_slot_of_seq:
                                    BA.write_wait("tensor", 1024 + fpar * 512, 1024 + (fpar + 1) * 512)
                                for c in range(NDC):
                                    R.op("tensor",
                                         (lambda c, fc, g, par, pu: lambda t: t.matmul(
                                             pu, U2[:, par, c, fc * P:(fc + 1) * P],
                                             XNT[:, c, g * G:(g + 1) * G],
                                             start=(c == 0), stop=(c == NDC - 1)))(c, fc, g, par, pu),
                                         inc="spe" if c == NDC - 1 else None)
                                u_pe = R.cnt["spe"]
                                R.wait("scalar", "spe", g_pe)
                                R.wait("scalar", "sdv", sil_rel[fpar])
                                R.op("scalar",
                                     (lambda fpar, pg: lambda a: a.activation(
                                         SIL[:, fpar, :], pg, AF.Silu))(fpar, pg),
                                     inc="sac")
                                sil_ac = R.cnt["sac"]
                                BA.reader(fpar * 512, (fpar + 1) * 512, "sac", sil_ac)
                                R.wait("vector", "sac", sil_ac)
                                R.wait("vector", "spe", u_pe)
                                R.op("vector",
                                     (lambda g, fc, fpar, pu: lambda v: v.tensor_tensor(
                                         out=H2[:, fc, :], in0=SIL[:, fpar, :],
                                         in1=pu, op=ALU.mult))(g, fc, fpar, pu),
                                     inc="sdv")
                                h_ready[(g, fc)] = R.cnt["sdv"]
                                hq.append(R.cnt["sdv"])
                                sil_rel[fpar] = R.cnt["sdv"]
                                BA.reader(1024 + fpar * 512, 1024 + (fpar + 1) * 512,
                                          "sdv", R.cnt["sdv"])
                            for j in range(4):
                                tt = g * 4 + j
                                dpar = tt % 2
                                pd = psB[:, dpar * D:(dpar + 1) * D]
                                BB.write_wait("tensor", dpar * D, (dpar + 1) * D)
                                for fc in range(nfc):
                                    if fc == 0:
                                        R.wait("tensor", "sdv", h_ready[(g, min(3, nfc - 1))])
                                    elif fc == 4:
                                        R.wait("tensor", "sdv", h_ready[(g, nfc - 1)])
                                    for h in range(2):
                                        R.op("tensor",
                                             (lambda fc, h, j, par, pd, nfc: lambda t: t.matmul(
                                                 pd[:, h * 512:(h + 1) * 512],
                                                 H2[:, fc, j * P:(j + 1) * P],
                                                 D2[:, par, fc, h * 512:(h + 1) * 512],
                                                 start=(fc == 0), stop=(fc == nfc - 1)))(fc, h, j, par, pd, nfc),
                                             inc="spe" if (fc == nfc - 1 and h == 1) else None)
                                d_pe = R.cnt["spe"]
                                R.wait("vector", "spe", d_pe)
                                R.op("vector",
                                     (lambda tt, e, pd: lambda v: v.scalar_tensor_tensor(
                                         out=OA[:, tt, :], in0=pd,
                                         scalar=EW[:, tt, e:e + 1], in1=OA[:, tt, :],
                                         op0=ALU.mult, op1=ALU.add))(tt, e, pd),
                                     inc="sdv")
                                BB.reader(dpar * D, (dpar + 1) * D, "sdv", R.cnt["sdv"])
                                if e == NEXP - 1 and fs0 == FS[-1][0] and j == 3:
                                    R.wait("sync", "sdv")
                                    R.op("sync",
                                         (lambda s, g: lambda gg: gg.dma_start(
                                             out=y[s].rearrange("(tt p) d -> p tt d", p=P)
                                             [:, g * 4:(g + 1) * 4, :],
                                             in_=OA[:, g * 4:(g + 1) * 4, :]))(s, g),
                                         inc="sy")
                                    y_group_done[g] = R.cnt["sy"]
                        if DBG and s == 0 and e == 0 and fs0 == 0:
                            R.wait("sync", "sdv")
                            R.op("sync", lambda g: g.dma_start(out=d_oaf, in_=OA[:]), inc="sy")
                            R.op("sync", lambda g: g.dma_start(out=d_h2, in_=H2[:]), inc="sy")
                            y_done = R.cnt["sy"]
                            R.wait("gpsimd", "sy", y_done)
                        gud_release[slot] = R.cnt["spe"]
                        slot += 1
                ffn_pe_end = R.cnt["spe"]
                ffn_dv_end = R.cnt["sdv"]
                ffn_ac_end = R.cnt["sac"]
                if s == 0:
                    wb_total = R.cnt["sb"]

            y_done = R.cnt["sy"]
            y_gd = dict(y_group_done)

        R.wait("sync", "sy", y_done)

        with nc.Block() as block:
            R.emit(block, sems)

    return nc


_NC_CACHE = None


def kernel(L0_attn_x, L0_attn_Wq, L0_attn_bq, L0_attn_Wk, L0_attn_bk,
           L0_attn_Wv, L0_attn_bv, L0_ffn_router_w, L0_ffn_gate_proj,
           L0_ffn_up_proj, L0_ffn_down_proj):
    global _NC_CACHE
    x = np.ascontiguousarray(np.asarray(L0_attn_x, dtype=np.float32))
    wq = np.asarray(L0_attn_Wq, dtype=np.float32)
    wk = np.asarray(L0_attn_Wk, dtype=np.float32)
    wv = np.asarray(L0_attn_Wv, dtype=np.float32)
    bq = np.asarray(L0_attn_bq, dtype=np.float32)
    bk = np.asarray(L0_attn_bk, dtype=np.float32)
    bv = np.asarray(L0_attn_bv, dtype=np.float32)
    rww = np.asarray(L0_ffn_router_w, dtype=np.float32)
    gp = np.ascontiguousarray(np.asarray(L0_ffn_gate_proj, dtype=np.float32))
    upw = np.ascontiguousarray(np.asarray(L0_ffn_up_proj, dtype=np.float32))
    dp = np.ascontiguousarray(np.asarray(L0_ffn_down_proj, dtype=np.float32))

    wqT = np.ascontiguousarray(wq.T)
    wkT = np.ascontiguousarray(wk.T)
    wvT = np.ascontiguousarray(wv.T)
    bqc = np.ascontiguousarray(bq.reshape(NDC, P).T)
    bkc = np.ascontiguousarray(bk.reshape(NDC, P).T)
    bvr = np.ascontiguousarray(bv.reshape(1, D))
    rwc = np.ascontiguousarray(rww.reshape(NDC, P, NEXP).transpose(1, 0, 2))
    ident = np.eye(P, dtype=np.float32)
    maskd = np.triu(np.full((P, P), -1e9, dtype=np.float32), k=1)
    onesd = np.ones((1, P), dtype=np.float32)

    if _NC_CACHE is None:
        _NC_CACHE = build_program()
    nc = _NC_CACHE

    shared = {"wqT": wqT, "wkT": wkT, "wvT": wvT, "bqc": bqc, "bkc": bkc,
              "bvr": bvr, "rw": rwc, "gate": gp, "up": upw, "down": dp,
              "ident": ident, "maskd": maskd, "onesd": onesd}
    in_maps = []
    for c in range(NCORES):
        m = dict(shared)
        m["x"] = x[c * NSEQ:(c + 1) * NSEQ]
        in_maps.append(m)
    res = run_bass_kernel_spmd(nc, in_maps, core_ids=list(range(NCORES)))
    out = np.concatenate([r["y"] for r in res.results], axis=0)
    return out


# version 16
# speedup vs baseline: 1.1727x; 1.1073x over previous
import numpy as np
from contextlib import ExitStack

import concourse.bass as bass
import concourse.mybir as mybir
from concourse.bass_utils import run_bass_kernel_spmd

F32 = mybir.dt.float32
BF16 = mybir.dt.bfloat16
AF = mybir.ActivationFunctionType
ALU = mybir.AluOpType
AX = mybir.AxisListType

P = 128
NSEQ = 4
T = 2048
NTT = T // P
D = 1024
NDC = D // P
F = 2816
NFC = F // P
NEXP = 8
NG = 4
G = 512
FS = [(0, 8), (8, 8), (16, 6)]
NCORES = 8

ENGINES = ("tensor", "vector", "scalar", "sync", "gpsimd")


class Rec:

    def __init__(self):
        self.items = {e: [] for e in ENGINES}
        self.cnt = {}
        self.step = {}

    def sem(self, name, step=1):
        self.cnt[name] = 0
        self.step[name] = step

    def op(self, eng, fn, inc=None, n=1):
        if inc is not None:
            self.cnt[inc] += self.step[inc] * n
        self.items[eng].append(("op", fn, inc, self.step[inc] * n if inc else 0))
        return self.cnt[inc] if inc else None

    def wait(self, eng, sname, val=None):
        v = self.cnt[sname] if val is None else val
        if v > 0:
            self.items[eng].append(("wait", sname, v))

    def emit(self, block, sems):
        hw = {e: {s: 0 for s in self.cnt} for e in ENGINES}

        def body(eng_name):
            def run(E):
                for it in self.items[eng_name]:
                    if it[0] == "wait":
                        _, s, v = it
                        if v > hw[eng_name][s]:
                            E.wait_ge(sems[s], v)
                            hw[eng_name][s] = v
                    else:
                        _, fn, inc, k = it
                        instr = fn(E)
                        if inc is not None:
                            instr.then_inc(sems[inc], k)
            return run

        block.tensor(body("tensor"))
        block.vector(body("vector"))
        block.scalar(body("scalar"))
        block.sync(body("sync"))
        block.gpsimd(body("gpsimd"))


class Banks:

    def __init__(self, R):
        self.R = R
        self.rel = [{"sdv": 0, "sac": 0} for _ in range(4)]

    def _banks(self, c0, c1):
        return range(c0 // 512, (c1 + 511) // 512)

    def reader(self, c0, c1, sem, val):
        for b in self._banks(c0, c1):
            self.rel[b][sem] = max(self.rel[b][sem], val)

    def write_wait(self, eng, c0, c1):
        for b in self._banks(c0, c1):
            for sem in ("sdv", "sac"):
                self.R.wait(eng, sem, self.rel[b][sem])


def build_program(DBG=False):
    nc = bass.Bass("TRN2", target_bir_lowering=False, debug=False)
    x = nc.dram_tensor("x", [NSEQ, T, D], F32, kind="ExternalInput").ap()
    wqT = nc.dram_tensor("wqT", [P, NDC, D], F32, kind="ExternalInput").ap()
    wkT = nc.dram_tensor("wkT", [P, NDC, D], F32, kind="ExternalInput").ap()
    wvT = nc.dram_tensor("wvT", [P, NDC, D], F32, kind="ExternalInput").ap()
    bqc = nc.dram_tensor("bqc", [P, NDC], F32, kind="ExternalInput").ap()
    bkc = nc.dram_tensor("bkc", [P, NDC], F32, kind="ExternalInput").ap()
    bvr = nc.dram_tensor("bvr", [1, D], F32, kind="ExternalInput").ap()
    rw = nc.dram_tensor("rw", [P, NDC, NEXP], F32, kind="ExternalInput").ap()
    gate = nc.dram_tensor("gate", [NEXP, P, NDC, F], F32, kind="ExternalInput").ap()
    up = nc.dram_tensor("up", [NEXP, P, NDC, F], F32, kind="ExternalInput").ap()
    down = nc.dram_tensor("down", [NEXP, P, NFC, D], F32, kind="ExternalInput").ap()
    ident = nc.dram_tensor("ident", [P, P], F32, kind="ExternalInput").ap()
    maskd = nc.dram_tensor("maskd", [P, P], F32, kind="ExternalInput").ap()
    onesd = nc.dram_tensor("onesd", [1, P], F32, kind="ExternalInput").ap()
    y = nc.dram_tensor("y", [NSEQ, T, D], F32, kind="ExternalOutput").ap()
    NSLOT = NEXP * len(FS)
    gsc = nc.dram_tensor("gsc", [NSLOT, P, NDC, 1024], BF16, kind="Internal").ap()
    usc = nc.dram_tensor("usc", [NSLOT, P, NDC, 1024], BF16, kind="Internal").ap()
    dsc = nc.dram_tensor("dsc", [NSLOT, P, NDC, 1024], BF16, kind="Internal").ap()
    if DBG:
        d_xnt = nc.dram_tensor("d_xnt", [P, NDC, T], BF16, kind="ExternalOutput").ap()
        d_stats = nc.dram_tensor("d_stats", [P, 3, NTT], F32, kind="ExternalOutput").ap()
        d_ew = nc.dram_tensor("d_ew", [P, NTT, NEXP], F32, kind="ExternalOutput").ap()
        d_kt = nc.dram_tensor("d_kt", [P, NDC, T], BF16, kind="ExternalOutput").ap()
        d_vt = nc.dram_tensor("d_vt", [P, NTT, D], BF16, kind="ExternalOutput").ap()
        d_qg = nc.dram_tensor("d_qg", [P, NDC, G], BF16, kind="ExternalOutput").ap()
        d_pb = nc.dram_tensor("d_pb", [P, T], BF16, kind="ExternalOutput").ap()
        d_smx = nc.dram_tensor("d_smx", [P, 2, 4], F32, kind="ExternalOutput").ap()
        d_oa = nc.dram_tensor("d_oa", [P, NTT, D], F32, kind="ExternalOutput").ap()
        d_oaf = nc.dram_tensor("d_oaf", [P, NTT, D], F32, kind="ExternalOutput").ap()
        d_h2 = nc.dram_tensor("d_h2", [P, NDC, G], BF16, kind="ExternalOutput").ap()
        d_x = nc.dram_tensor("d_x", [P, NTT, D], F32, kind="ExternalOutput").ap()

    R = Rec()
    for sname in ("sx", "sy", "sw", "sl", "sb"):
        R.sem(sname, 16)
    for sname in ("spe", "sdv", "sac"):
        R.sem(sname, 1)

    with ExitStack() as ctx:
        sb = lambda name, shape, dt: ctx.enter_context(nc.sbuf_tensor(name, shape, dt))
        OA = sb("OA", [P, NTT, D], F32)
        XNT = sb("XNT", [P, NDC, T], BF16)
        sum16 = sb("sum16", [P, NTT], F32)
        ss16 = sb("ss16", [P, NTT], F32)
        mean16 = sb("mean16", [P, NTT], F32)
        var16 = sb("var16", [P, NTT], F32)
        rstd16 = sb("rstd16", [P, NTT], F32)
        nmr16 = sb("nmr16", [P, NTT], F32)
        tmp16 = sb("tmp16", [P, NTT], F32)
        EW = sb("EW", [P, NTT, NEXP], F32)
        EQ1 = sb("EQ1", [P, NTT, NEXP], BF16)
        EQ2 = sb("EQ2", [P, NTT, NEXP], BF16)
        DLT = sb("DLT", [P, NTT], F32)
        W1 = sb("W1", [P, NTT], F32)
        W2 = sb("W2", [P, NTT], F32)
        lgt = sb("lgt", [P, NEXP], F32)
        l2b = sb("l2b", [P, NEXP], F32)
        m1b = sb("m1b", [P, 1], F32)
        m2b = sb("m2b", [P, 1], F32)
        smx = sb("smx", [P, 2, 4], F32)
        nmx = sb("nmx", [P, 2, 4], F32)
        scn = sb("scn", [P, 2, 4], F32)
        identT = sb("identT", [P, P], BF16)
        maskT = sb("maskT", [P, P], BF16)
        RW = sb("RW", [P, NDC, NEXP], BF16)
        bqs = sb("bqs", [P, NDC], F32)
        bks = sb("bks", [P, NDC], F32)
        WINB = 108544
        arena = ctx.enter_context(nc.sbuf_tensor("win_arena", [P, WINB], mybir.dt.uint8))
        abase = nc.lookup_mloc(arena).addr
        at = lambda name, shape, dt, off: nc.alloc_sbuf_tensor_at(
            name, shape, dt, offset=abase + off)
        KT = at("KT", [P, NDC, T], BF16, 0)
        W8b = at("W8b", [P, NDC, D], BF16, 81920)
        VT = at("VT", [P, NTT, D], BF16, 32768)
        W8 = at("W8", [P, NDC, D], BF16, 65536)
        QG = at("QG", [P, NDC, G], BF16, 81920)
        Pb = at("Pb", [P, T], BF16, 90112)
        PT = at("PT", [P, NTT, P], BF16, 94208)
        G2 = at("G2", [P, 2, NDC, 1024], BF16, 0)
        U2 = at("U2", [P, 2, NDC, 1024], BF16, 32768)
        D2 = at("D2", [P, 2, NDC, 1024], BF16, 65536)
        H2 = at("H2", [P, NDC, G], BF16, 98304)
        SIL = at("SIL", [P, 2, 512], BF16, 106496)
        xn_tok = at("xn_tok", [P, 2, D], BF16, 0)
        dumpA = at("dumpA", [P, D], BF16, 4096)
        bvs = at("bvs", [1, D], BF16, 98304)
        ones1 = at("ones1", [1, P], BF16, 100352)
        psA = ctx.enter_context(nc.psum_tensor("psA", [P, 2048], F32))
        psB = ctx.enter_context(nc.psum_tensor("psB", [P, 2048], F32))
        psAb = psA.bitcast(BF16)

        sems = {}
        for sname in R.cnt:
            sems[sname] = ctx.enter_context(nc.semaphore(sname))

        BA = Banks(R)
        BB = Banks(R)

        R.op("gpsimd", lambda g: g.dma_start(out=identT[:], in_=ident), inc="sl")
        R.op("gpsimd", lambda g: g.dma_start(out=RW[:], in_=rw), inc="sl")
        R.op("gpsimd", lambda g: g.dma_start(out=maskT[:], in_=maskd), inc="sl")
        R.op("sync", lambda g: g.dma_start(out=bqs[:], in_=bqc), inc="sl")
        R.op("sync", lambda g: g.dma_start(out=bks[:], in_=bkc), inc="sl")
        consts_done = R.cnt["sl"]
        for e in ("tensor", "vector", "scalar"):
            R.wait(e, "sl", consts_done)

        ffn_pe_end = 0
        ffn_dv_end = 0
        ffn_ac_end = 0
        y_done = 0
        y_gd = {}
        wb_done = {}
        wb_total = 0

        for s in range(NSEQ):
            xgload = []
            for g in range(4):
                if s > 0:
                    R.wait("sync", "sy", y_gd.get(g, y_done))
                R.op("sync",
                     (lambda s, g: lambda gg: gg.dma_start(
                         out=OA[:, g * 4:(g + 1) * 4, :],
                         in_=x[s].rearrange("(tt p) d -> p tt d", p=P)
                         [:, g * 4:(g + 1) * 4, :]))(s, g),
                     inc="sx")
                xgload.append(R.cnt["sx"])
            xload = [xgload[tt // 4] for tt in range(NTT)]

            grp_nmr_dv = {}
            if DBG and s == 0:
                R.wait("sync", "sx", xload[-1])
                R.op("sync", lambda g: g.dma_start(out=d_x, in_=OA[:]), inc="sy")
            xnt_done = []
            rlogs = [psB[:, 1024:1024 + NEXP], psB[:, 1536:1536 + NEXP]]
            tr_pe_prev = 0
            rt_pe_by_tt = {}

            def top2_chain(tt):
                rlog = rlogs[tt % 2]
                c0 = 1024 + (tt % 2) * 512
                R.wait("vector", "spe", rt_pe_by_tt[tt])
                R.op("vector", (lambda rlog: lambda v: v.tensor_copy(lgt[:], rlog))(rlog),
                     inc="sdv")
                BB.reader(c0, c0 + NEXP, "sdv", R.cnt["sdv"])
                R.wait("vector", "sdv")
                R.op("vector", lambda v: v.reduce_max(m1b[:], lgt[:], axis=AX.X), inc="sdv")
                R.wait("vector", "sdv")
                R.op("vector",
                     (lambda tt: lambda v: v.tensor_scalar(
                         out=EQ1[:, tt, :], in0=lgt[:], scalar1=m1b[:], scalar2=None,
                         op0=ALU.is_equal, op1=ALU.bypass))(tt), inc="sdv")
                R.wait("vector", "sdv")
                R.op("vector",
                     (lambda tt: lambda v: v.scalar_tensor_tensor(
                         out=l2b[:], in0=EQ1[:, tt, :], scalar=-1e9, in1=lgt[:],
                         op0=ALU.mult, op1=ALU.add))(tt), inc="sdv")
                R.wait("vector", "sdv")
                R.op("vector", lambda v: v.reduce_max(m2b[:], l2b[:], axis=AX.X), inc="sdv")
                R.wait("vector", "sdv")
                R.op("vector",
                     (lambda tt: lambda v: v.tensor_scalar(
                         out=EQ2[:, tt, :], in0=l2b[:], scalar1=m2b[:], scalar2=None,
                         op0=ALU.is_equal, op1=ALU.bypass))(tt), inc="sdv")
                R.op("vector",
                     (lambda tt: lambda v: v.tensor_tensor(
                         out=DLT[:, tt:tt + 1], in0=m2b[:], in1=m1b[:],
                         op=ALU.subtract))(tt), inc="sdv")

            def a_tile_pipeline(tt):
                nonlocal tr_pe_prev
                par = tt % 2
                R.wait("scalar", "sdv", grp_nmr_dv[tt // 4])
                if tt >= 2:
                    R.wait("scalar", "spe", tr_pe_prev)
                R.op("scalar",
                     (lambda tt, par: lambda a: a.activation(
                         xn_tok[:, par, :], OA[:, tt, :], AF.Identity,
                         scale=rstd16[:, tt:tt + 1],
                         bias=nmr16[:, tt:tt + 1]))(tt, par),
                     inc="sac")
                xn_dv = R.cnt["sac"]
                R.wait("tensor", "sac", xn_dv)
                BA.write_wait("tensor", 0, 512)
                for c in range(NDC):
                    R.op("tensor",
                         (lambda c, par: lambda t: t.transpose(
                             psAb[:, c * P:(c + 1) * P],
                             xn_tok[:, par, c * P:(c + 1) * P], identT[:]))(c, par),
                         inc="spe" if c == NDC - 1 else None)
                tr_pe_prev = R.cnt["spe"]
                R.wait("vector", "spe", tr_pe_prev)
                R.op("vector",
                     (lambda tt: lambda v: v.tensor_copy(
                         XNT[:, :, tt * P:(tt + 1) * P],
                         psAb[:, 0:NDC * P].rearrange("p (c n) -> p c n", c=NDC)))(tt),
                     inc="sdv")
                xnt_done.append(("sdv", R.cnt["sdv"]))
                BA.reader(0, 512, "sdv", R.cnt["sdv"])
                R.wait("tensor", xnt_done[-1][0], xnt_done[-1][1])
                c0 = 1024 + (tt % 2) * 512
                BB.write_wait("tensor", c0, c0 + NEXP)
                for c in range(NDC):
                    R.op("tensor",
                         (lambda c, tt, rlog: lambda t: t.matmul(
                             rlog, XNT[:, c, tt * P:(tt + 1) * P], RW[:, c, :],
                             start=(c == 0), stop=(c == NDC - 1)))(c, tt, rlogs[tt % 2]),
                         inc="spe" if c == NDC - 1 else None)
                rt_pe_by_tt[tt] = R.cnt["spe"]
                if tt >= 1:
                    top2_chain(tt - 1)

            for gA in range(4):
                g0 = gA * 4
                for tt in range(g0, g0 + 4):
                    dumpv = XNT[:, :, tt * P:(tt + 1) * P]
                    R.wait("vector", "sx", xgload[gA])
                    R.op("vector",
                         (lambda tt, dumpv: lambda v: v.scalar_tensor_tensor(
                             out=dumpv, in0=OA[:, tt, :], scalar=1.0, in1=OA[:, tt, :],
                             op0=ALU.bypass, op1=ALU.mult,
                             accum_out=ss16[:, tt:tt + 1]))(tt, dumpv),
                         inc="sdv")
                    R.wait("scalar", "sx", xgload[gA])
                    R.op("scalar",
                         (lambda tt: lambda a: a.activation(
                             dumpA[:], OA[:, tt, :], AF.Identity,
                             accum_out=sum16[:, tt:tt + 1]))(tt),
                         inc="sac")
                sl4 = slice(g0, g0 + 4)
                R.wait("vector", "sdv")
                R.wait("vector", "sac")
                R.op("vector", (lambda sl4: lambda v: v.tensor_scalar(
                    out=mean16[:, sl4], in0=sum16[:, sl4], scalar1=1.0 / D, scalar2=None,
                    op0=ALU.mult, op1=ALU.bypass))(sl4), inc="sdv")
                R.wait("vector", "sdv")
                R.op("vector", (lambda sl4: lambda v: v.scalar_tensor_tensor(
                    out=tmp16[:, sl4], in0=mean16[:, sl4], scalar=1.0, in1=mean16[:, sl4],
                    op0=ALU.bypass, op1=ALU.mult))(sl4), inc="sdv")
                R.wait("vector", "sdv")
                R.op("vector", (lambda sl4: lambda v: v.scalar_tensor_tensor(
                    out=var16[:, sl4], in0=ss16[:, sl4], scalar=1.0 / D, in1=tmp16[:, sl4],
                    op0=ALU.mult, op1=ALU.subtract))(sl4), inc="sdv")
                R.wait("vector", "sdv")
                R.op("vector", (lambda sl4: lambda v: v.tensor_scalar(
                    out=var16[:, sl4], in0=var16[:, sl4], scalar1=1e-5, scalar2=None,
                    op0=ALU.add, op1=ALU.bypass))(sl4), inc="sdv")
                R.wait("scalar", "sdv")
                R.op("scalar", (lambda sl4: lambda a: a.activation(
                    tmp16[:, sl4], var16[:, sl4], AF.Sqrt))(sl4), inc="sac")
                R.wait("vector", "sac")
                R.op("vector", (lambda sl4: lambda v: v.reciprocal(
                    rstd16[:, sl4], tmp16[:, sl4]))(sl4), inc="sdv")
                R.wait("vector", "sdv")
                R.op("vector", (lambda sl4: lambda v: v.scalar_tensor_tensor(
                    out=nmr16[:, sl4], in0=mean16[:, sl4], scalar=-1.0, in1=rstd16[:, sl4],
                    op0=ALU.mult, op1=ALU.mult))(sl4), inc="sdv")
                grp_nmr_dv[gA] = R.cnt["sdv"]
                for tt in range(g0, g0 + 4):
                    a_tile_pipeline(tt)
            top2_chain(NTT - 1)
            R.wait("scalar", "sdv")
            R.op("scalar", lambda a: a.activation(W1[:], DLT[:], AF.Exp), inc="sac")
            R.wait("vector", "sac")
            R.op("vector", lambda v: v.tensor_scalar(
                out=W1[:], in0=W1[:], scalar1=1.0, scalar2=None,
                op0=ALU.add, op1=ALU.bypass), inc="sdv")
            R.wait("vector", "sdv")
            R.op("vector", lambda v: v.reciprocal(W1[:], W1[:]), inc="sdv")
            R.wait("vector", "sdv")
            R.op("vector", lambda v: v.tensor_scalar(
                out=W2[:], in0=W1[:], scalar1=-1.0, scalar2=1.0,
                op0=ALU.mult, op1=ALU.add), inc="sdv")
            R.wait("vector", "sdv")
            for tt in range(NTT):
                R.op("vector",
                     (lambda tt: lambda v: v.tensor_scalar(
                         out=EW[:, tt, :], in0=EQ1[:, tt, :], scalar1=W1[:, tt:tt + 1],
                         scalar2=None, op0=ALU.mult, op1=ALU.bypass))(tt), inc="sdv")
                R.wait("vector", "sdv")
                R.op("vector",
                     (lambda tt: lambda v: v.scalar_tensor_tensor(
                         out=EW[:, tt, :], in0=EQ2[:, tt, :], scalar=W2[:, tt:tt + 1],
                         in1=EW[:, tt, :], op0=ALU.mult, op1=ALU.add))(tt), inc="sdv")
                R.wait("vector", "sdv")
            a_end_dv = R.cnt["sdv"]
            if DBG and s == 0:
                R.wait("sync", "sdv", a_end_dv)
                R.op("sync", lambda g: g.dma_start(out=d_xnt, in_=XNT[:]), inc="sy")
                R.op("sync", lambda g: g.dma_start(out=d_stats[:, 0, :], in_=mean16[:]), inc="sy")
                R.op("sync", lambda g: g.dma_start(out=d_stats[:, 1, :], in_=rstd16[:]), inc="sy")
                R.op("sync", lambda g: g.dma_start(out=d_stats[:, 2, :], in_=var16[:]), inc="sy")
                R.op("sync", lambda g: g.dma_start(out=d_ew, in_=EW[:]), inc="sy")
                y_done = R.cnt["sy"]

            if True:
                R.wait("gpsimd", "spe", ffn_pe_end)
                R.wait("gpsimd", "sdv", ffn_dv_end)
                R.wait("gpsimd", "sac", ffn_ac_end)
                R.wait("gpsimd", "sl", consts_done)
                if s == 1:
                    R.wait("gpsimd", "sb", wb_total)
                R.op("gpsimd", lambda g: g.dma_start(
                    out=W8[:], in_=wkT), inc="sw")
                R.op("gpsimd", lambda g: g.dma_start(out=bvs[:], in_=bvr), inc="sw")
                R.op("gpsimd", lambda g: g.dma_start(out=ones1[:], in_=onesd), inc="sw")
                R.op("gpsimd", lambda g: g.dma_start(
                    out=W8b[:], in_=wvT), inc="sw")
                wk_ld = R.cnt["sw"]
                wv_ld = wk_ld
                R.wait("tensor", "sw", wk_ld)
                for ec in range(NDC):
                    for kb in range(NG):
                        R.wait("tensor", xnt_done[kb * 4 + 3][0], xnt_done[kb * 4 + 3][1])
                        par = (ec * NG + kb) % 2
                        pb = psB[:, par * 512:(par + 1) * 512]
                        BB.write_wait("tensor", par * 512, (par + 1) * 512)
                        for c in range(NDC):
                            R.op("tensor",
                                 (lambda c, ec, kb, pb: lambda t: t.matmul(
                                     pb, W8[:, c, ec * P:(ec + 1) * P],
                                     XNT[:, c, kb * G:(kb + 1) * G],
                                     start=(c == 0), stop=(c == NDC - 1)))(c, ec, kb, pb),
                                 inc="spe" if c == NDC - 1 else None)
                        R.wait("scalar", "spe")
                        R.op("scalar",
                             (lambda ec, kb, pb: lambda a: a.activation(
                                 KT[:, ec, kb * G:(kb + 1) * G], pb, AF.Identity,
                                 bias=bks[:, ec:ec + 1]))(ec, kb, pb), inc="sac")
                        BB.reader(par * 512, (par + 1) * 512, "sac", R.cnt["sac"])
                k_pe_end = R.cnt["spe"]
                R.wait("gpsimd", "spe", k_pe_end)
                R.op("gpsimd", lambda g: g.dma_start(
                    out=W8[:], in_=wqT), inc="sw")
                wq_ld = R.cnt["sw"]

                R.wait("tensor", "sw", wv_ld)
                for tt in range(NTT):
                    par = tt % 2
                    pv = psB[:, par * D:(par + 1) * D]
                    BB.write_wait("tensor", par * D, (par + 1) * D)
                    for h in range(2):
                        for c in range(NDC):
                            R.op("tensor",
                                 (lambda c, tt, h, pv: lambda t: t.matmul(
                                     pv[:, h * 512:(h + 1) * 512],
                                     XNT[:, c, tt * P:(tt + 1) * P],
                                     W8b[:, c, h * 512:(h + 1) * 512],
                                     start=(c == 0), stop=False))(c, tt, h, pv),
                                 inc=None)
                        R.op("tensor",
                             (lambda tt, h, pv: lambda t: t.matmul(
                                 pv[:, h * 512:(h + 1) * 512], ones1[:],
                                 bvs[:, h * 512:(h + 1) * 512],
                                 start=False, stop=True))(tt, h, pv),
                             inc="spe" if h == 1 else None)
                    R.wait("vector", "spe")
                    R.op("vector",
                         (lambda tt, pv: lambda v: v.tensor_copy(VT[:, tt, :], pv))(tt, pv),
                         inc="sdv")
                    BB.reader(par * D, (par + 1) * D, "sdv", R.cnt["sdv"])
                v_pe_end = R.cnt["spe"]
                if DBG and s == 0:
                    R.wait("sync", "sac")
                    R.wait("sync", "sdv")
                    R.op("sync", lambda g: g.dma_start(out=d_kt, in_=KT[:]), inc="sy")
                    R.op("sync", lambda g: g.dma_start(out=d_vt, in_=VT[:]), inc="sy")
                    y_done = R.cnt["sy"]

                R.wait("tensor", "sw", wq_ld)
                for g in range(NG):
                    for ec in range(NDC):
                        par = ec % 2
                        pq = psA[:, par * 512:(par + 1) * 512]
                        BA.write_wait("tensor", par * 512, (par + 1) * 512)
                        for c in range(NDC):
                            R.op("tensor",
                                 (lambda c, ec, g, pq: lambda t: t.matmul(
                                     pq, W8[:, c, ec * P:(ec + 1) * P],
                                     XNT[:, c, g * G:(g + 1) * G],
                                     start=(c == 0), stop=(c == NDC - 1)))(c, ec, g, pq),
                                 inc="spe" if c == NDC - 1 else None)
                        R.wait("scalar", "spe")
                        R.op("scalar",
                             (lambda ec, pq: lambda a: a.activation(
                                 QG[:, ec, :], pq, AF.Identity,
                                 bias=bqs[:, ec:ec + 1]))(ec, pq), inc="sac")
                        BA.reader(par * 512, (par + 1) * 512, "sac", R.cnt["sac"])
                    q_ready = R.cnt["sac"]

                    for j in range(4):
                        Tq = g * 4 + j
                        klen = (Tq + 1) * P
                        nb = (klen + 511) // 512
                        par = Tq % 2
                        R.wait("tensor", "sac", q_ready)
                        BA.write_wait("tensor", 0, klen)
                        blk_pe = []
                        for kb in range(nb):
                            ncols = min(512, klen - kb * 512)
                            is_diag = (kb == nb - 1)
                            for c in range(NDC):
                                R.op("tensor",
                                     (lambda c, kb, ncols, g, j, is_diag: lambda t: t.matmul(
                                         psA[:, kb * 512:kb * 512 + ncols],
                                         QG[:, c, j * P:(j + 1) * P],
                                         KT[:, c, kb * 512:kb * 512 + ncols],
                                         start=(c == 0),
                                         stop=(c == NDC - 1 and not is_diag)))(c, kb, ncols, g, j, is_diag),
                                     inc="spe" if (c == NDC - 1 and not is_diag) else None)
                            if is_diag:
                                R.op("tensor",
                                     (lambda Tq, kb, ncols: lambda t: t.matmul(
                                         psA[:, kb * 512 + ncols - P:kb * 512 + ncols],
                                         identT[:], maskT[:],
                                         start=False, stop=True))(Tq, kb, ncols),
                                     inc="spe")
                            blk_pe.append(R.cnt["spe"])
                        s_pe = R.cnt["spe"]
                        for kb in range(nb):
                            ncols = min(512, klen - kb * 512)
                            R.wait("vector", "spe", blk_pe[kb])
                            R.op("vector",
                                 (lambda kb, ncols, par: lambda v: v.reduce_max(
                                     nmx[:, par, kb:kb + 1],
                                     psA[:, kb * 512:kb * 512 + ncols], axis=AX.X,
                                     negate=True))(kb, ncols, par), inc="sdv")
                            BA.reader(kb * 512, kb * 512 + ncols, "sdv", R.cnt["sdv"])
                        R.wait("vector", "sdv")
                        R.op("vector",
                             (lambda nb, par: lambda v: v.tensor_scalar(
                                 out=scn[:, par, 0:nb], in0=nmx[:, par, 0:nb],
                                 scalar1=1.0 / 32.0, scalar2=None,
                                 op0=ALU.mult, op1=ALU.min,
                                 accum_out=smx[:, par, 1:2]))(nb, par), inc="sdv")
                        mx_dv = R.cnt["sdv"]
                        nkb = Tq + 1
                        exp_ac_blk = []
                        R.wait("scalar", "sdv", mx_dv)
                        for kb in range(nb):
                            ncols = min(512, klen - kb * 512)
                            R.op("scalar",
                                 (lambda kb, ncols, par: lambda a: a.activation(
                                     Pb[:, kb * 512:kb * 512 + ncols],
                                     psA[:, kb * 512:kb * 512 + ncols], AF.Exp,
                                     bias=smx[:, par, 1:2], scale=1.0 / 32.0,
                                     accum_out=nmx[:, par, kb:kb + 1]))(kb, ncols, par),
                                 inc="sac")
                            exp_ac_blk.append(R.cnt["sac"])
                            BA.reader(kb * 512, kb * 512 + ncols, "sac", R.cnt["sac"])
                        exp_ac = R.cnt["sac"]
                        R.wait("vector", "sac", exp_ac)
                        R.op("vector",
                             (lambda nb, par: lambda v: v.tensor_scalar(
                                 out=scn[:, par, 0:nb], in0=nmx[:, par, 0:nb],
                                 scalar1=0.0, scalar2=None,
                                 op0=ALU.add, op1=ALU.add,
                                 accum_out=smx[:, par, 2:3]))(nb, par), inc="sdv")
                        R.wait("vector", "sdv")
                        R.op("vector",
                             (lambda par: lambda v: v.reciprocal(
                                 smx[:, par, 3:4], smx[:, par, 2:3]))(par), inc="sdv")
                        recip_dv = R.cnt["sdv"]
                        BA.write_wait("tensor", 0, nkb * P // 2)
                        copy_dv = []
                        for kb in range(nkb):
                            R.wait("tensor", "sac", exp_ac_blk[kb // 4])
                            R.op("tensor",
                                 (lambda kb: lambda t: t.transpose(
                                     psAb[:, kb * P:(kb + 1) * P],
                                     Pb[:, kb * P:(kb + 1) * P], identT[:]))(kb),
                                 inc="spe" if (kb % 8 == 7 or kb == nkb - 1) else None)
                            if kb % 8 == 7 or kb == nkb - 1:
                                b0 = (kb // 8) * 8
                                cntk = kb + 1 - b0
                                R.wait("vector", "spe")
                                R.op("vector",
                                     (lambda b0, cntk: lambda v: v.tensor_copy(
                                         PT[:, b0:b0 + cntk, :],
                                         psAb[:, b0 * P:(b0 + cntk) * P]
                                         .rearrange("p (k n) -> p k n", n=P)))(b0, cntk),
                                     inc="sdv")
                                copy_dv.append(R.cnt["sdv"])
                                BA.reader(b0 * P // 2, (b0 + cntk) * P // 2, "sdv", R.cnt["sdv"])
                        tr_pe = R.cnt["spe"]
                        pav = psB[:, par * D:(par + 1) * D]
                        BB.write_wait("tensor", par * D, (par + 1) * D)
                        for kb in range(nkb):
                            if kb % 8 == 0:
                                R.wait("tensor", "sdv",
                                       copy_dv[min(kb // 8, len(copy_dv) - 1)])
                            for h in range(2):
                                R.op("tensor",
                                     (lambda kb, h, pav, nkb: lambda t: t.matmul(
                                         pav[:, h * 512:(h + 1) * 512], PT[:, kb, :],
                                         VT[:, kb, h * 512:(h + 1) * 512],
                                         start=(kb == 0), stop=(kb == nkb - 1)))(kb, h, pav, nkb),
                                     inc="spe" if (kb == nkb - 1 and h == 1) else None)
                        av_pe = R.cnt["spe"]
                        R.wait("vector", "spe", av_pe)
                        R.wait("vector", "sdv", recip_dv)
                        R.op("vector",
                             (lambda Tq, par, pav: lambda v: v.scalar_tensor_tensor(
                                 out=OA[:, Tq, :], in0=pav, scalar=smx[:, par, 3:4],
                                 in1=OA[:, Tq, :], op0=ALU.mult, op1=ALU.add))(Tq, par, pav),
                             inc="sdv")
                        BB.reader(par * D, (par + 1) * D, "sdv", R.cnt["sdv"])
                c_pe_end = R.cnt["spe"]
                c_dv_end = R.cnt["sdv"]
                c_ac_end = R.cnt["sac"]
                if DBG and s == 0:
                    R.wait("sync", "sdv", c_dv_end)
                    R.wait("sync", "sac", c_ac_end)
                    R.op("sync", lambda g: g.dma_start(out=d_qg, in_=QG[:]), inc="sy")
                    R.op("sync", lambda g: g.dma_start(out=d_pb, in_=Pb[:]), inc="sy")
                    R.op("sync", lambda g: g.dma_start(out=d_smx, in_=smx[:]), inc="sy")
                    R.op("sync", lambda g: g.dma_start(out=d_oa, in_=OA[:]), inc="sy")
                    y_done = R.cnt["sy"]

            if True:
                gud_release = {}
                y_group_done = {}
                hq = []
                sil_rel = {0: 0, 1: 0}
                slot = 0
                first_ffn = True
                w_ready_prev = 0
                for e in range(NEXP):
                    for (fs0, nfc) in FS:
                        par = slot % 2
                        if slot <= 1:
                            R.wait("gpsimd", "spe", c_pe_end)
                            R.wait("gpsimd", "sdv", c_dv_end)
                            R.wait("gpsimd", "sac", c_ac_end)
                            if DBG:
                                R.wait("gpsimd", "sy", y_done)
                        else:
                            R.wait("gpsimd", "spe", gud_release[slot - 2])
                        if s == 0 and slot >= 2:
                            R.wait("gpsimd", "sb", wb_done[slot - 2])
                        if s == 1 and slot <= 1:
                            R.wait("gpsimd", "sb", wb_total)
                        R.wait("gpsimd", "sw", w_ready_prev)
                        halves = ([(0, 2), (2, 4), (4, 6), (6, nfc)]
                                  if slot == 0 else [(0, nfc)])
                        w_half = []
                        for (h0, h1) in halves:
                            if s == 0:
                                R.op("gpsimd",
                                     (lambda e, fs0, par, h0, h1: lambda g: g.dma_start(
                                         out=G2[:, par, :, h0 * P:h1 * P],
                                         in_=gate[e]
                                         [:, :, (fs0 + h0) * P:(fs0 + h1) * P]))(e, fs0, par, h0, h1),
                                     inc="sw")
                                R.op("gpsimd",
                                     (lambda e, fs0, par, h0, h1: lambda g: g.dma_start(
                                         out=U2[:, par, :, h0 * P:h1 * P],
                                         in_=up[e]
                                         [:, :, (fs0 + h0) * P:(fs0 + h1) * P]))(e, fs0, par, h0, h1),
                                     inc="sw")
                                R.op("gpsimd",
                                     (lambda e, fs0, par, h0, h1: lambda g: g.dma_start(
                                         out=D2[:, par, h0:h1, :],
                                         in_=down[e]
                                         [:, fs0 + h0:fs0 + h1, :]))(e, fs0, par, h0, h1),
                                     inc="sw")
                            else:
                                R.op("gpsimd",
                                     (lambda sl_, par, h0, h1: lambda g: g.dma_start(
                                         out=G2[:, par, :, h0 * P:h1 * P],
                                         in_=gsc[sl_][:, :, h0 * P:h1 * P]))(slot, par, h0, h1),
                                     inc="sw")
                                R.op("gpsimd",
                                     (lambda sl_, par, h0, h1: lambda g: g.dma_start(
                                         out=U2[:, par, :, h0 * P:h1 * P],
                                         in_=usc[sl_][:, :, h0 * P:h1 * P]))(slot, par, h0, h1),
                                     inc="sw")
                                R.op("gpsimd",
                                     (lambda sl_, par, h0, h1: lambda g: g.dma_start(
                                         out=D2[:, par, h0:h1, :],
                                         in_=dsc[sl_][:, h0:h1, :]))(slot, par, h0, h1),
                                     inc="sw")
                            w_half.append((h1, R.cnt["sw"]))
                        w_ready = R.cnt["sw"]
                        w_ready_prev = w_ready
                        if s == 0:
                            R.wait("sync", "sw", w_ready)
                            R.op("sync",
                                 (lambda sl_, par, nfc: lambda g: g.dma_start(
                                     out=gsc[sl_][:, :, 0:nfc * P],
                                     in_=G2[:, par, :, 0:nfc * P]))(slot, par, nfc),
                                 inc="sb")
                            R.op("sync",
                                 (lambda sl_, par, nfc: lambda g: g.dma_start(
                                     out=usc[sl_][:, :, 0:nfc * P],
                                     in_=U2[:, par, :, 0:nfc * P]))(slot, par, nfc),
                                 inc="sb")
                            R.op("sync",
                                 (lambda sl_, par, nfc: lambda g: g.dma_start(
                                     out=dsc[sl_][:, 0:nfc, :],
                                     in_=D2[:, par, 0:nfc, :]))(slot, par, nfc),
                                 inc="sb")
                            wb_done[slot] = R.cnt["sb"]
                        first_slot_of_seq = (slot == 0)

                        def w_wait_for_fc(fc):
                            for (h1, cnt) in w_half:
                                if fc < h1:
                                    return cnt
                            return w_ready
                        if first_ffn:
                            R.wait("tensor", "sdv", c_dv_end)
                            R.wait("tensor", "sac", c_ac_end)
                            first_ffn = False
                        h_ready = {}
                        for g in range(NG):
                            gpar = g % 2
                            for fc in range(nfc):
                                fpar = fc % 2
                                pg = psA[:, fpar * 512:(fpar + 1) * 512]
                                pu = psA[:, 1024 + fpar * 512:1024 + (fpar + 1) * 512]
                                R.wait("tensor", "sw", w_wait_for_fc(fc))
                                if len(hq) >= 2:
                                    R.wait("tensor", "sdv", hq[-2])
                                elif first_slot_of_seq:
                                    BA.write_wait("tensor", fpar * 512, (fpar + 1) * 512)
                                for c in range(NDC):
                                    R.op("tensor",
                                         (lambda c, fc, g, par, pg: lambda t: t.matmul(
                                             pg, G2[:, par, c, fc * P:(fc + 1) * P],
                                             XNT[:, c, g * G:(g + 1) * G],
                                             start=(c == 0), stop=(c == NDC - 1)))(c, fc, g, par, pg),
                                         inc="spe" if c == NDC - 1 else None)
                                g_pe = R.cnt["spe"]
                                if len(hq) < 2 and first_slot_of_seq:
                                    BA.write_wait("tensor", 1024 + fpar * 512, 1024 + (fpar + 1) * 512)
                                for c in range(NDC):
                                    R.op("tensor",
                                         (lambda c, fc, g, par, pu: lambda t: t.matmul(
                                             pu, U2[:, par, c, fc * P:(fc + 1) * P],
                                             XNT[:, c, g * G:(g + 1) * G],
                                             start=(c == 0), stop=(c == NDC - 1)))(c, fc, g, par, pu),
                                         inc="spe" if c == NDC - 1 else None)
                                u_pe = R.cnt["spe"]
                                R.wait("scalar", "spe", g_pe)
                                R.wait("scalar", "sdv", sil_rel[fpar])
                                R.op("scalar",
                                     (lambda fpar, pg: lambda a: a.activation(
                                         SIL[:, fpar, :], pg, AF.Silu))(fpar, pg),
                                     inc="sac")
                                sil_ac = R.cnt["sac"]
                                BA.reader(fpar * 512, (fpar + 1) * 512, "sac", sil_ac)
                                R.wait("vector", "sac", sil_ac)
                                R.wait("vector", "spe", u_pe)
                                R.op("vector",
                                     (lambda g, fc, fpar, pu: lambda v: v.tensor_tensor(
                                         out=H2[:, fc, :], in0=SIL[:, fpar, :],
                                         in1=pu, op=ALU.mult))(g, fc, fpar, pu),
                                     inc="sdv")
                                h_ready[(g, fc)] = R.cnt["sdv"]
                                hq.append(R.cnt["sdv"])
                                sil_rel[fpar] = R.cnt["sdv"]
                                BA.reader(1024 + fpar * 512, 1024 + (fpar + 1) * 512,
                                          "sdv", R.cnt["sdv"])
                            for j in range(4):
                                tt = g * 4 + j
                                dpar = tt % 2
                                pd = psB[:, dpar * D:(dpar + 1) * D]
                                BB.write_wait("tensor", dpar * D, (dpar + 1) * D)
                                for fc in range(nfc):
                                    if fc == 0:
                                        R.wait("tensor", "sdv", h_ready[(g, min(3, nfc - 1))])
                                    elif fc == 4:
                                        R.wait("tensor", "sdv", h_ready[(g, nfc - 1)])
                                    for h in range(2):
                                        R.op("tensor",
                                             (lambda fc, h, j, par, pd, nfc: lambda t: t.matmul(
                                                 pd[:, h * 512:(h + 1) * 512],
                                                 H2[:, fc, j * P:(j + 1) * P],
                                                 D2[:, par, fc, h * 512:(h + 1) * 512],
                                                 start=(fc == 0), stop=(fc == nfc - 1)))(fc, h, j, par, pd, nfc),
                                             inc="spe" if (fc == nfc - 1 and h == 1) else None)
                                d_pe = R.cnt["spe"]
                                R.wait("vector", "spe", d_pe)
                                R.op("vector",
                                     (lambda tt, e, pd: lambda v: v.scalar_tensor_tensor(
                                         out=OA[:, tt, :], in0=pd,
                                         scalar=EW[:, tt, e:e + 1], in1=OA[:, tt, :],
                                         op0=ALU.mult, op1=ALU.add))(tt, e, pd),
                                     inc="sdv")
                                BB.reader(dpar * D, (dpar + 1) * D, "sdv", R.cnt["sdv"])
                                if e == NEXP - 1 and fs0 == FS[-1][0] and j == 3:
                                    R.wait("sync", "sdv")
                                    R.op("sync",
                                         (lambda s, g: lambda gg: gg.dma_start(
                                             out=y[s].rearrange("(tt p) d -> p tt d", p=P)
                                             [:, g * 4:(g + 1) * 4, :],
                                             in_=OA[:, g * 4:(g + 1) * 4, :]))(s, g),
                                         inc="sy")
                                    y_group_done[g] = R.cnt["sy"]
                        if DBG and s == 0 and e == 0 and fs0 == 0:
                            R.wait("sync", "sdv")
                            R.op("sync", lambda g: g.dma_start(out=d_oaf, in_=OA[:]), inc="sy")
                            R.op("sync", lambda g: g.dma_start(out=d_h2, in_=H2[:]), inc="sy")
                            y_done = R.cnt["sy"]
                            R.wait("gpsimd", "sy", y_done)
                        gud_release[slot] = R.cnt["spe"]
                        slot += 1
                ffn_pe_end = R.cnt["spe"]
                ffn_dv_end = R.cnt["sdv"]
                ffn_ac_end = R.cnt["sac"]
                if s == 0:
                    wb_total = R.cnt["sb"]

            y_done = R.cnt["sy"]
            y_gd = dict(y_group_done)

        R.wait("sync", "sy", y_done)

        with nc.Block() as block:
            R.emit(block, sems)

    return nc


_NC_CACHE = None


def kernel(L0_attn_x, L0_attn_Wq, L0_attn_bq, L0_attn_Wk, L0_attn_bk,
           L0_attn_Wv, L0_attn_bv, L0_ffn_router_w, L0_ffn_gate_proj,
           L0_ffn_up_proj, L0_ffn_down_proj):
    global _NC_CACHE
    x = np.ascontiguousarray(np.asarray(L0_attn_x, dtype=np.float32))
    wq = np.asarray(L0_attn_Wq, dtype=np.float32)
    wk = np.asarray(L0_attn_Wk, dtype=np.float32)
    wv = np.asarray(L0_attn_Wv, dtype=np.float32)
    bq = np.asarray(L0_attn_bq, dtype=np.float32)
    bk = np.asarray(L0_attn_bk, dtype=np.float32)
    bv = np.asarray(L0_attn_bv, dtype=np.float32)
    rww = np.asarray(L0_ffn_router_w, dtype=np.float32)
    gp = np.ascontiguousarray(np.asarray(L0_ffn_gate_proj, dtype=np.float32))
    upw = np.ascontiguousarray(np.asarray(L0_ffn_up_proj, dtype=np.float32))
    dp = np.ascontiguousarray(np.asarray(L0_ffn_down_proj, dtype=np.float32))

    wqT = np.ascontiguousarray(wq.T.reshape(NDC, P, D).transpose(1, 0, 2))
    wkT = np.ascontiguousarray(wk.T.reshape(NDC, P, D).transpose(1, 0, 2))
    wvT = np.ascontiguousarray(wv.T.reshape(NDC, P, D).transpose(1, 0, 2))
    gp = np.ascontiguousarray(gp.reshape(NEXP, NDC, P, F).transpose(0, 2, 1, 3))
    upw = np.ascontiguousarray(upw.reshape(NEXP, NDC, P, F).transpose(0, 2, 1, 3))
    dp = np.ascontiguousarray(dp.reshape(NEXP, NFC, P, D).transpose(0, 2, 1, 3))
    bqc = np.ascontiguousarray(bq.reshape(NDC, P).T)
    bkc = np.ascontiguousarray(bk.reshape(NDC, P).T)
    bvr = np.ascontiguousarray(bv.reshape(1, D))
    rwc = np.ascontiguousarray(rww.reshape(NDC, P, NEXP).transpose(1, 0, 2))
    ident = np.eye(P, dtype=np.float32)
    maskd = np.triu(np.full((P, P), -1e9, dtype=np.float32), k=1)
    onesd = np.ones((1, P), dtype=np.float32)

    if _NC_CACHE is None:
        _NC_CACHE = build_program()
    nc = _NC_CACHE

    shared = {"wqT": wqT, "wkT": wkT, "wvT": wvT, "bqc": bqc, "bkc": bkc,
              "bvr": bvr, "rw": rwc, "gate": gp, "up": upw, "down": dp,
              "ident": ident, "maskd": maskd, "onesd": onesd}
    in_maps = []
    for c in range(NCORES):
        m = dict(shared)
        m["x"] = x[c * NSEQ:(c + 1) * NSEQ]
        in_maps.append(m)
    res = run_bass_kernel_spmd(nc, in_maps, core_ids=list(range(NCORES)))
    out = np.concatenate([r["y"] for r in res.results], axis=0)
    return out
